# version 07
# baseline (speedup 1.0000x reference)
import numpy as np
import ml_dtypes
import concourse.bass as bass
import concourse.mybir as mybir
from concourse.bass_utils import run_bass_kernel_spmd
from contextlib import ExitStack

F32 = mybir.dt.float32
F32R = mybir.dt.float32r
BF16 = mybir.dt.bfloat16
AF = mybir.ActivationFunctionType
OP = mybir.AluOpType
AX = mybir.AxisListType

NCORES = 8
DM = 1024
T = 2048
TC = 512
NTC = T // TC
Q = 128
DIN = 2048
DST = 128
NH = 32
HD = 64
CONV_DIM = DIN + 2 * DST
DIP = DIN + CONV_DIM + NH
NKC = DM // 128
EPS = 1e-5

ENGS = ("tensor", "vector", "scalar", "sync", "gpsimd")


class Bld:

    NPOOL = 12

    def __init__(self):
        self.stream = {e: [] for e in ENGS}
        self.cnt = {e: 0 for e in ENGS}
        self.dma_seq = {"sync": 0, "gpsimd": 0}
        self.dma_slot_cnt = {}
        self.engine_vc = {e: {} for e in ENGS}
        self.h_vc = {}

    @staticmethod
    def _flat(deps):
        out = []
        for d in deps:
            if d is None:
                continue
            if isinstance(d, list):
                out.extend(Bld._flat(d))
            else:
                out.append(d)
        return out

    def _merge(self, eng_or_q, deps):
        vc = dict(self.engine_vc[eng_or_q])
        for d in deps:
            for s, v in self.h_vc[d].items():
                if vc.get(s, 0) < v:
                    vc[s] = v
        return vc

    def op(self, eng, fn, deps=(), inc=True):
        deps = self._flat(deps)
        vc = self._merge(eng, deps)
        self.engine_vc[eng] = vc
        waits = dict(vc)
        if inc:
            self.cnt[eng] += 1
            h = (eng, self.cnt[eng])
            vc = dict(vc)
            vc[eng] = self.cnt[eng]
            self.h_vc[h] = vc
            self.engine_vc[eng] = vc
        else:
            h = None
        self.stream[eng].append((waits, fn, "eng" if inc else None))
        return h

    def dma(self, q, out, in_, deps=()):
        deps = self._flat(deps)
        slot = self.dma_seq[q] % self.NPOOL
        self.dma_seq[q] += 1
        key = f"dma_{q}_{slot}"
        prev = self.dma_slot_cnt.get(key, 0)
        vc = self._merge(q, deps)
        if prev:
            if vc.get(key, 0) < 16 * prev:
                vc[key] = 16 * prev
        self.engine_vc[q] = vc
        waits = dict(vc)
        self.dma_slot_cnt[key] = prev + 1
        h = (key, 16 * self.dma_slot_cnt[key])
        hvc = dict(vc)
        hvc[key] = 16 * self.dma_slot_cnt[key]
        self.h_vc[h] = hvc

        def fn(eng, _o=out, _i=in_):
            return eng.dma_start(out=_o, in_=_i)

        self.stream[q].append((waits, fn, ("dma", key)))
        return h

    def emit(self, nc, block, sems, strict=True):
        for eng_name in ENGS:
            stream = self.stream[eng_name]

            def body(eng, _stream=stream, _eng_name=eng_name):
                cur = {}
                nown = 0
                for waits, fn, kind in _stream:
                    for sem_name in sorted(waits):
                        val = waits[sem_name]
                        if (not strict and sem_name == _eng_name
                                and _eng_name in ("tensor",)):
                            continue
                        if cur.get(sem_name, 0) >= val:
                            continue
                        eng.wait_ge(sems[sem_name], val)
                        cur[sem_name] = val
                    inst = fn(eng)
                    if kind == "eng":
                        assert inst is not None
                        inst.then_inc(sems[_eng_name], 1)
                        nown += 1
                    elif isinstance(kind, tuple) and kind[0] == "dma":
                        inst.then_inc(sems[kind[1]], 16)

            getattr(block, eng_name)(body)


class Psum:

    def __init__(self):
        self.writer = [None] * 8
        self.readers = [[] for _ in range(8)]

    def wdeps(self, bank):
        d = list(self.readers[bank])
        if self.writer[bank] is not None:
            d.append(self.writer[bank])
        return d

    def wrote(self, bank, h):
        self.writer[bank] = h
        self.readers[bank] = []

    def rdeps(self, bank):
        return [self.writer[bank]] if self.writer[bank] is not None else []

    def read(self, bank, h):
        self.readers[bank].append(h)


class Arena:

    def __init__(self, tensor, nelem, name):
        self.t = tensor
        self.n = nelem
        self.name = name
        self.off = 0
        self.marks = []

    def alloc(self, shape):
        free = 1
        for s in shape[1:]:
            free *= s
        off = self.off
        assert off + free <= self.n, f"arena {self.name} overflow: {off}+{free}>{self.n}"
        self.off = off + free
        return self.view(off, shape)

    def view(self, off, shape):
        ap = [[self.n, shape[0]]]
        strides = []
        acc = 1
        for s in reversed(shape[1:]):
            strides.append(acc)
            acc *= s
        strides.reverse()
        for s, st in zip(shape[1:], strides):
            ap.append([st, s])
        return bass.AP(self.t, off, ap)

    def push(self):
        self.marks.append(self.off)

    def pop(self):
        self.off = self.marks.pop()


def fview(ap, dtype):
    return ap.bitcast(dtype)



DEBUG_TAPS = ()


def build_program():
    nc = bass.Bass("TRN2", target_bir_lowering=False)

    d_x = nc.dram_tensor("x_t", [DM, T], F32, kind="ExternalInput")
    d_win = nc.dram_tensor("in_proj_w", [DM, DIP], F32, kind="ExternalInput")
    d_convw = nc.dram_tensor("conv_w_t", [CONV_DIM, 4], F32, kind="ExternalInput")
    d_convb = nc.dram_tensor("conv_b_t", [128, 18], F32, kind="ExternalInput")
    d_normw = nc.dram_tensor("norm_w_t", [128, 16], F32, kind="ExternalInput")
    d_wout = nc.dram_tensor("out_proj_w", [DIN, DM], F32, kind="ExternalInput")
    d_dtb = nc.dram_tensor("dt_bias", [NH, 1], F32, kind="ExternalInput")
    d_alog = nc.dram_tensor("A_log", [NH, 1], F32, kind="ExternalInput")
    d_D = nc.dram_tensor("D_rep", [128, 16], F32, kind="ExternalInput")
    d_eu = nc.dram_tensor("expert_usage", [8, 1], F32, kind="ExternalInput")
    d_gw1 = nc.dram_tensor("gate_w1", [DM, DM], F32, kind="ExternalInput")
    d_gb1 = nc.dram_tensor("gate_b1_rep", [128, 8], F32, kind="ExternalInput")
    d_lnsc = nc.dram_tensor("ln_scale_rep", [128, 8], F32, kind="ExternalInput")
    d_lnof = nc.dram_tensor("ln_offset_rep", [128, 8], F32, kind="ExternalInput")
    d_gw2 = nc.dram_tensor("gate_w2", [DM, 512], F32, kind="ExternalInput")
    d_gb2 = nc.dram_tensor("gate_b2_rep", [128, 4], F32, kind="ExternalInput")
    d_gw3 = nc.dram_tensor("gate_w3", [512, 8], F32, kind="ExternalInput")
    d_gb3 = nc.dram_tensor("gate_b3_row", [1, 8], F32, kind="ExternalInput")
    d_affw = nc.dram_tensor("aff_w", [DM, 8], F32, kind="ExternalInput")
    d_affb = nc.dram_tensor("aff_b", [8, 1], F32, kind="ExternalInput")
    d_e0w1 = nc.dram_tensor("e0_w1", [3, DM, 4096], F32, kind="ExternalInput")
    d_e0w2 = nc.dram_tensor("e0_w2", [3, 4096, DM], F32, kind="ExternalInput")
    d_e1w1 = nc.dram_tensor("e1_w1", [3, DM, DIN], F32, kind="ExternalInput")
    d_e1w2 = nc.dram_tensor("e1_w2", [3, DIN, DIN], F32, kind="ExternalInput")
    d_e1w3 = nc.dram_tensor("e1_w3", [3, DIN, DM], F32, kind="ExternalInput")
    d_e2w1 = nc.dram_tensor("e2_w1", [2, DM, 3072], F32, kind="ExternalInput")
    d_e2w2 = nc.dram_tensor("e2_w2", [2, 3072, DM], F32, kind="ExternalInput")
    d_noise = nc.dram_tensor("noise", [T, 8], F32, kind="ExternalInput")
    d_ident = nc.dram_tensor("ident128", [128, 128], F32, kind="ExternalInput")
    d_trimask = nc.dram_tensor("trimask", [128, 128], F32, kind="ExternalInput")
    d_sel = nc.dram_tensor("sel_heads", [NH, 16 * 128], BF16, kind="ExternalInput")
    d_ktarg = nc.dram_tensor("ktarg", [128, 2], F32, kind="ExternalInput")
    d_out = nc.dram_tensor("out_t", [8, 128, T], F32, kind="ExternalOutput")

    d_imp_in = nc.dram_tensor("imp_in", [1, T], F32)
    d_imp_out = nc.dram_tensor("imp_out", [NCORES, T], F32)
    d_l1 = nc.dram_tensor("l1_scratch", [1, T], F32)
    d_x1 = nc.dram_tensor("x1_scratch", [128, NKC * T], F32)
    d_sel8 = nc.dram_tensor("sel8", [8, 8 * 128], BF16, kind="ExternalInput")
    d_e0b1 = nc.dram_tensor("e0_b1_rep", [128, 3, 32], F32, kind="ExternalInput")
    d_e0b2 = nc.dram_tensor("e0_b2_rep", [128, 3, 8], F32, kind="ExternalInput")
    d_e1b1 = nc.dram_tensor("e1_b1_rep", [128, 3, 16], F32, kind="ExternalInput")
    d_e1b2 = nc.dram_tensor("e1_b2_rep", [128, 3, 16], F32, kind="ExternalInput")
    d_e1b3 = nc.dram_tensor("e1_b3_rep", [128, 3, 8], F32, kind="ExternalInput")
    d_e2b1 = nc.dram_tensor("e2_b1_rep", [128, 2, 24], F32, kind="ExternalInput")
    d_e2b2 = nc.dram_tensor("e2_b2_rep", [128, 2, 8], F32, kind="ExternalInput")

    b = Bld()
    ps = Psum()
    tap_d = {}
    for tn, tshape in DEBUG_TAPS:
        tap_d[tn] = nc.dram_tensor("tap_" + tn, tshape, F32, kind="ExternalOutput")

    with ExitStack() as stk:
        e = stk.enter_context
        x_fm = e(nc.sbuf_tensor("x_fm", [128, NKC, T], F32))
        af32_t = e(nc.sbuf_tensor("af32", [128, 4096], F32))
        ab16_t = e(nc.sbuf_tensor("ab16", [128, 45312], BF16))
        af = Arena(af32_t, 4096, "f32")
        ab = Arena(ab16_t, 45312, "b16")

        ones_col = e(nc.sbuf_tensor("ones_col", [128, 1], F32))
        ones_row = e(nc.sbuf_tensor("ones_row", [1, 128], F32))
        ones_sq = e(nc.sbuf_tensor("ones_sq", [128, 128], F32))
        ident = e(nc.sbuf_tensor("ident", [128, 128], F32))
        trimask = e(nc.sbuf_tensor("trimask_s", [128, 128], F32))
        sel = e(nc.sbuf_tensor("sel_s", [NH, 16, 128], BF16))
        convw = e(nc.sbuf_tensor("convw_s", [128, 18, 4], F32))
        convb = e(nc.sbuf_tensor("convb_s", [128, 18], F32))
        normw = e(nc.sbuf_tensor("normw_s", [128, 16], F32))
        drep = e(nc.sbuf_tensor("drep_s", [128, 16], F32))
        dtb = e(nc.sbuf_tensor("dtb_s", [NH, 1], F32))
        acol = e(nc.sbuf_tensor("acol_s", [NH, 1], F32))
        lnsc = e(nc.sbuf_tensor("lnsc_s", [128, 8], F32))
        lnof = e(nc.sbuf_tensor("lnof_s", [128, 8], F32))
        gb1 = e(nc.sbuf_tensor("gb1_s", [128, 8], F32))
        gb2 = e(nc.sbuf_tensor("gb2_s", [128, 4], F32))
        gb3bc = e(nc.sbuf_tensor("gb3bc_s", [128, 8], F32))
        gb3row = e(nc.sbuf_tensor("gb3row_s", [1, 8], F32))
        affb = e(nc.sbuf_tensor("affb_s", [8, 1], F32))
        eu_sb = e(nc.sbuf_tensor("eu_s", [8, 1], F32))
        noise_sb = e(nc.sbuf_tensor("noise_s", [128, 16, 8], F32))
        ktarg = e(nc.sbuf_tensor("ktarg_s", [128, 2], F32))
        impall = e(nc.sbuf_tensor("impall_s", [128, 128], F32))
        bis = e(nc.sbuf_tensor("bis_s", [128, 16], F32))
        wrows = e(nc.sbuf_tensor("wrows_s", [8, T], BF16))
        affbc = e(nc.sbuf_tensor("affbc_s", [128, 8], F32))
        nsc = e(nc.sbuf_tensor("nsc_s", [128, 1], F32))
        thr_sb = e(nc.sbuf_tensor("thr_s", [1, 8], F32))
        scl_sb = e(nc.sbuf_tensor("scl_s", [1, 8], F32))

        ln1_t = e(nc.sbuf_tensor("ln1_t", [128, NKC, TC], F32R))
        winbuf_t = e(nc.sbuf_tensor("winbuf_t", [128, 2, NKC, 128], F32R))
        pb = [e(nc.psum_tensor(f"pb{i}", [128, 512], F32)) for i in range(8)]

        sems = {}
        for s in ENGS:
            sems[s] = e(nc.semaphore("sem_" + s))
        for q in ("sync", "gpsimd"):
            for i in range(Bld.NPOOL):
                sems[f"dma_{q}_{i}"] = e(nc.semaphore(f"sem_dma_{q}_{i}"))
        cc_sem = e(nc.semaphore("sem_cc"))

        block = e(nc.Block())

        ln1 = ln1_t[:, :, :]
        winbuf = [winbuf_t[:, 0, :, :], winbuf_t[:, 1, :, :]]
        dtrow = af.alloc([NH, TC])
        crow = af.alloc([NH, TC])
        tcm = af.alloc([NH, 128])
        urow = af.alloc([NH, 128])
        emrow = af.alloc([1, 32])
        bbrow = af.alloc([1, 32])
        wmat = af.alloc([128, 32])
        emat = af.alloc([128, 32])
        bbmat = af.alloc([128, 32])
        row_a = af.alloc([1, TC])
        row_b = af.alloc([1, TC])
        row_c = af.alloc([1, TC])
        af.push()

        o_sz = ab.off
        sz = ab.alloc([128, 16, TC])
        xbc = ab.alloc([128, 18, TC])
        raw = [ab.alloc([128, 520]) for _ in range(2)]
        o_g = ab.off
        g_sb = ab.alloc([128, 16, TC])
        smat = ab.alloc([128, DIN])
        xpp = ab.alloc([128, DIN])
        xppb = ab.alloc([128, DIN])
        o_wout = ab.off
        woutbuf = [ab.alloc([128, 16, 128]) for _ in range(2)]
        yscr = ab.alloc([128, DIN])
        o_vfull = ab.off
        vfull = ab.alloc([128, 16, 128])
        gmref = ab.alloc([128, 128])
        btm = ab.alloc([128, 128])
        o_vbf = ab.off
        vbf = ab.alloc([NH, 128])
        halo = ab.alloc([128, 18, 4])
        b16_amark = ab.off

        FR = lambda ap: fview(ap, F32R)

        ld0 = []
        ld0.append(b.dma("sync", ident[:, :], d_ident[:, :]))
        ld0.append(b.dma("sync", trimask[:, :], d_trimask[:, :]))
        ld0.append(b.dma("sync", sel[:, :, :],
                         bass.AP(d_sel, 0, [[16 * 128, NH], [128, 16], [1, 128]])))
        ld0.append(b.dma("sync", convw[:, :, :],
                         bass.AP(d_convw, 0, [[4, 128], [128 * 4, 18], [1, 4]])))
        ld0.append(b.dma("sync", convb[:, :], d_convb[:, :]))
        ld0.append(b.dma("sync", normw[:, :], d_normw[:, :]))
        ld0.append(b.dma("sync", drep[:, :], d_D[:, :]))
        ld0.append(b.dma("sync", dtb[:, :], d_dtb[:, :]))
        ld0.append(b.dma("sync", acol[:, :], d_alog[:, :]))
        ld0.append(b.dma("sync", lnsc[:, :], d_lnsc[:, :]))
        ld0.append(b.dma("sync", lnof[:, :], d_lnof[:, :]))
        ld0.append(b.dma("sync", gb1[:, :], d_gb1[:, :]))
        ld0.append(b.dma("sync", gb2[:, :], d_gb2[:, :]))
        ld0.append(b.dma("sync", gb3row[:, :], d_gb3[:, :]))
        ld0.append(b.dma("sync", affb[:, :], d_affb[:, :]))
        ld0.append(b.dma("sync", eu_sb[:, :], d_eu[:, :]))
        ld0.append(b.dma("sync", ktarg[:, :], d_ktarg[:, :]))
        ld0.append(b.dma("sync", noise_sb[:, :, :],
                         bass.AP(d_noise, 0, [[8, 128], [128 * 8, 16], [1, 8]])))
        h_x = b.dma("sync", x_fm[:, :, :],
                    bass.AP(d_x, 0, [[T, 128], [128 * T, NKC], [1, T]]))

        h_ones = b.op("vector", lambda v: v.memset(ones_col[:, :], 1.0))
        h_onesrow = b.op("vector", lambda v: v.memset(ones_row[:, :], 1.0))
        h_onesq = b.op("vector", lambda v: v.memset(ones_sq[:, :], 1.0))
        h_halo0 = b.op("vector", lambda v: v.memset(halo[:, :, :], 0.0))
        h_smat0 = b.op("vector", lambda v: v.memset(smat[:, :], 0.0))
        h_acol = b.op("scalar", lambda s: s.activation(acol[:, :], acol[:, :], AF.Exp),
                      deps=[ld0[8]])
        h_acol = b.op("vector", lambda v: v.tensor_scalar_mul(acol[:, :], acol[:, :], -1.0),
                      deps=[h_acol])
        h = b.op("tensor", lambda t: t.matmul(pb[6][:, :8], ones_row[:, :], gb3row[:, :],
                                              start=True, stop=True),
                 deps=[ld0[13], h_onesrow] + ps.wdeps(6))
        ps.wrote(6, h)
        h_gb3 = b.op("vector", lambda v: v.tensor_copy(gb3bc[:, :], pb[6][:, :8]),
                     deps=ps.rdeps(6))
        ps.read(6, h_gb3)

        dttmp = af.alloc([NH, TC])
        zrow = af.alloc([NH, 128])
        c2 = af.alloc([NH, 2])
        identb = ab.alloc([128, 128])
        onescb = ab.alloc([128, 1])
        o_sqscr = ab.off
        sqscr = ab.alloc([128, 2, TC])
        o_cacc = ab.off
        convacc = ab.alloc([128, 2, TC])

        h_zrow = b.op("vector", lambda v: v.memset(zrow, 0.0))
        h_identb = b.op("vector", lambda v: v.tensor_copy(identb, ident[:, :]), deps=[ld0[0]])
        h_onescb = b.op("vector", lambda v: v.memset(onescb, 1.0))

        free = {}

        def win_src(mi):
            c0 = 128 * mi
            w = 128 if mi < 34 else 32
            return bass.AP(d_win, c0, [[DIP, 128], [128 * DIP, NKC], [1, w]]), w

        def wout_src(mi):
            c0 = 128 * mi
            return bass.AP(d_wout, c0, [[DM, 128], [128 * DM, 16], [1, 128]])

        LNW = 1.0 / DM
        hs_dS = {"valid": False}

        def emit_lnstats(src_slices, nk, scale, bank_s, bank_q, deps_in, ones_bf=False):
            w = src_slices[0].shape[-1]
            o_lhs = onescb if ones_bf else ones_col
            o_dep = h_onescb if ones_bf else h_ones
            hsum = None
            for k in range(nk):
                hsum = b.op("tensor",
                            lambda t, _k=k, _w=w: t.matmul(
                                pb[bank_s][:1, :_w], o_lhs[:, :], src_slices[_k],
                                start=(_k == 0), stop=(_k == nk - 1)),
                            deps=(ps.wdeps(bank_s) + deps_in + [o_dep]) if k == 0 else [],
                            inc=(k == nk - 1))
            ps.wrote(bank_s, hsum)
            hsq = None
            for k in range(nk):
                hq = b.op("scalar",
                          lambda s, _k=k, _w=w: s.activation(
                              sqscr[:, _k % 2, :_w], src_slices[_k], AF.Square),
                          deps=[free.get(f"sq{k % 2}")] + (deps_in if k < 2 else []))
                hsq = b.op("tensor",
                           lambda t, _k=k, _w=w: t.matmul(
                               pb[bank_q][:1, :_w], onescb, sqscr[:, _k % 2, :_w],
                               start=(_k == 0), stop=(_k == nk - 1)),
                           deps=([hq] + ps.wdeps(bank_q) + [h_onescb]) if k == 0 else [hq],
                           inc=True)
                free[f"sq{k % 2}"] = hsq
            ps.wrote(bank_q, hsq)
            return hsum, hsq

        def emit_ln_rows(bank_s, bank_q, n, eps):
            w = TC
            h1 = b.op("vector", lambda v: v.tensor_scalar(
                row_a[:, :], pb[bank_s][:1, :], 1.0 / n, None, OP.mult),
                deps=ps.rdeps(bank_s) + [free.get("rows_free")])
            ps.read(bank_s, h1)
            h2 = b.op("vector", lambda v: v.tensor_mul(row_b[:, :], row_a[:, :], row_a[:, :]),
                      deps=[h1])
            h3 = b.op("vector", lambda v: v.scalar_tensor_tensor(
                row_b[:, :], pb[bank_q][:1, :], 1.0 / n, row_b[:, :], OP.mult, OP.subtract),
                deps=ps.rdeps(bank_q) + [h2])
            ps.read(bank_q, h3)
            h4 = b.op("vector", lambda v: v.tensor_scalar_add(row_b[:, :], row_b[:, :], eps),
                      deps=[h3])
            h5 = b.op("vector", lambda v: v.reciprocal(row_b[:, :], row_b[:, :]), deps=[h4])
            h6 = b.op("scalar", lambda s: s.activation(row_b[:, :], row_b[:, :], AF.Sqrt),
                      deps=[h5])
            h7 = b.op("vector", lambda v: v.tensor_scalar(
                row_c[:, :], row_a[:, :], -1.0, None, OP.mult), deps=[h1])
            h8 = b.op("vector", lambda v: v.tensor_mul(row_c[:, :], row_c[:, :], row_b[:, :]),
                      deps=[h6, h7])
            return h6, h8

        def emit_bcast_row(row_ap, bank, deps_in, width=TC):
            h = b.op("tensor", lambda t: t.matmul(
                pb[bank][:, :width], ones_row[:, :], row_ap, start=True, stop=True),
                deps=ps.wdeps(bank) + deps_in + [h_onesrow])
            ps.wrote(bank, h)
            return h

        for tci in range(NTC):
            t0 = tci * TC
            xs = [x_fm[:, k, t0:t0 + TC] for k in range(NKC)]
            hstat = emit_lnstats(xs, NKC, LNW, 0, 1, [h_x])
            hrows = emit_ln_rows(0, 1, DM, EPS)
            if tci == 0 and tap_d:
                if "rowa" in tap_d:
                    free["tapr"] = [b.dma("gpsimd", tap_d["rowa"].ap(), row_a[:, :], deps=list(hrows)),
                                    b.dma("gpsimd", tap_d["rowb"].ap(), row_b[:, :], deps=list(hrows)),
                                    b.dma("gpsimd", tap_d["rowc"].ap(), row_c[:, :], deps=list(hrows))]
            hr = emit_bcast_row(row_b[:, :], 0, [hrows[0], free.get("tapr")])
            hn = emit_bcast_row(row_c[:, :], 1, [hrows[1], free.get("tapr")])
            free["rows_free"] = [hr, hn]
            ln1_done = []
            for k in range(NKC):
                ha = b.op("vector", lambda v, _k=k, _t0=t0: v.tensor_tensor(
                    FR(ln1[:, _k, :]), x_fm[:, _k, _t0:_t0 + TC], pb[0][:, :], OP.mult),
                    deps=ps.rdeps(0) + [free.get("ln1f")])
                ps.read(0, ha)
                hb = b.op("vector", lambda v, _k=k: v.tensor_tensor(
                    FR(ln1[:, _k, :]), FR(ln1[:, _k, :]), pb[1][:, :], OP.add),
                    deps=ps.rdeps(1) + [ha])
                ps.read(1, hb)
                ln1_done.append(hb)

            wload = {}
            for mi in range(2):
                srcap, wdt = win_src(mi)
                wload[mi] = b.dma("gpsimd", FR(winbuf[mi % 2][:, :, :wdt]), srcap,
                                  deps=[free.get(f"winb{mi % 2}")])
            banks_ip = [2, 3, 0, 1]
            sz_done = []
            for mi in range(35):
                bank = banks_ip[mi % 4]
                wdt = 128 if mi < 34 else 32
                dmm = ps.wdeps(bank) + [wload[mi]] + ln1_done
                hmm = None
                for k in range(NKC):
                    hmm = b.op("tensor", lambda t, _mi=mi, _k=k, _b=bank, _w=wdt: t.matmul(
                        pb[_b][:_w, :], FR(winbuf[_mi % 2][:, _k, :_w]), FR(ln1[:, _k, :]),
                        start=(_k == 0), stop=(_k == NKC - 1)),
                        deps=dmm if k == 0 else [], inc=(k == NKC - 1))
                ps.wrote(bank, hmm)
                free[f"winb{mi % 2}"] = hmm
                if mi + 2 < 35:
                    srcap2, wdt2 = win_src(mi + 2)
                    wload[mi + 2] = b.dma("gpsimd", FR(winbuf[(mi + 2) % 2][:, :, :wdt2]),
                                          srcap2, deps=[hmm])
                if mi < 16:
                    hv = b.op("scalar", lambda s, _mi=mi, _b=bank: s.activation(
                        sz[:, _mi, :], pb[_b][:, :], AF.Silu),
                        deps=ps.rdeps(bank) + [free.get(f"sz{mi}")])
                    ps.read(bank, hv)
                    free[f"sz{mi}"] = None
                    sz_done.append(hv)
                elif mi < 34:
                    ci = mi - 16
                    rb = ci % 2
                    he = b.op("vector", lambda v, _b=bank, _rb=rb: v.tensor_copy(
                        raw[_rb][:, 3:3 + TC], pb[_b][:, :]),
                        deps=ps.rdeps(bank) + [free.get(f"raw{rb}")])
                    ps.read(bank, he)
                    hh = b.op("vector", lambda v, _ci=ci, _rb=rb: v.tensor_copy(
                        raw[_rb][:, 0:3], halo[:, _ci, 0:3]),
                        deps=[he, h_halo0, free.get(f"halo{ci}")])
                    ab_ = ci % 2
                    h1 = b.op("vector", lambda v, _ci=ci, _rb=rb, _a=ab_: v.tensor_scalar(
                        convacc[:, _a, :], raw[_rb][:, 0:TC], convw[:, _ci, 0:1], None, OP.mult),
                        deps=[hh, ld0[3], free.get(f"cacc{ab_}")])
                    for kk in range(1, 4):
                        h1 = b.op("vector", lambda v, _ci=ci, _kk=kk, _rb=rb, _a=ab_:
                                  v.scalar_tensor_tensor(
                                      convacc[:, _a, :], raw[_rb][:, _kk:_kk + TC],
                                      convw[:, _ci, _kk:_kk + 1], convacc[:, _a, :],
                                      OP.mult, OP.add), deps=[h1])
                    hu = b.op("vector", lambda v, _ci=ci, _rb=rb: v.tensor_copy(
                        halo[:, _ci, 0:3], raw[_rb][:, TC:TC + 3]), deps=[he, hh])
                    free[f"halo{ci}"] = hu
                    free[f"raw{rb}"] = hu
                    hsl = b.op("scalar", lambda s, _ci=ci, _a=ab_: s.activation(
                        xbc[:, _ci, :], convacc[:, _a, :], AF.Silu,
                        bias=convb[:, _ci:_ci + 1]),
                        deps=[h1, ld0[4], free.get(f"xbcf{ci}")])
                    free[f"cacc{ab_}"] = hsl
                    free[f"xbc{ci}"] = hsl
                else:
                    hdt = b.op("scalar", lambda s, _b=bank: s.activation(
                        dttmp[:, :], pb[_b][:32, :], AF.Exp, bias=dtb[:, :]),
                        deps=ps.rdeps(bank) + [ld0[7], free.get("dtrow")])
                    ps.read(bank, hdt)
                    hdt = b.op("scalar", lambda s: s.activation(
                        dtrow[:, :], dttmp[:, :], AF.Ln, bias=1.0), deps=[hdt])
                    free["dt_ready"] = hdt
                    free["ln1f"] = hmm
            hal = b.op("vector", lambda v: v.tensor_scalar(
                dttmp[:, :], dtrow[:, :], acol[:, :], None, OP.mult),
                deps=[free["dt_ready"], h_acol])
            c_ready = []
            for ql in range(4):
                hc = b.op("vector", lambda v, _q=ql: v.tensor_tensor_scan(
                    crow[:, _q * 128:(_q + 1) * 128], dttmp[:, _q * 128:(_q + 1) * 128],
                    zrow, 0.0, OP.add, OP.add), deps=[hal, h_zrow])
                c_ready.append(hc)

            def fb(base, extra, dims):
                return bass.AP(base.tensor, base.offset + extra, [list(base.ap[0])] + dims)

            for ql in range(4):
                gq = tci * 4 + ql
                qs = ql * 128
                mcol = crow[:, qs + 63:qs + 64]
                h_tcm = b.op("vector", lambda v, _q=qs: v.tensor_scalar(
                    tcm[:, :], crow[:, _q:_q + 128], crow[:, _q + 63:_q + 64], None,
                    OP.subtract), deps=[c_ready[ql], free.get("tcm_free")])
                h_vbf = b.op("scalar", lambda s: s.activation(vbf[:, :], tcm[:, :], AF.Exp),
                             deps=[h_tcm, free.get("vbf_free")])
                h_ur = b.op("scalar", lambda s: s.activation(urow[:, :], tcm[:, :], AF.Exp,
                                                             scale=-1.0),
                            deps=[h_tcm, free.get("urow_free")])
                h_ur = b.op("vector", lambda v, _q=qs: v.tensor_tensor(
                    urow[:, :], urow[:, :], dtrow[:, _q:_q + 128], OP.mult), deps=[h_ur])
                free["tcm_free"] = h_ur
                h_c2a = b.op("vector", lambda v, _q=qs: v.tensor_copy(
                    c2[:, 0:1], crow[:, _q + 63:_q + 64]), deps=[c_ready[ql], free.get("c2f")])
                h_c2b = b.op("vector", lambda v, _q=qs: v.tensor_scalar(
                    c2[:, 1:2], crow[:, _q + 127:_q + 128], crow[:, _q + 63:_q + 64], None,
                    OP.subtract), deps=[c_ready[ql], h_c2a])
                hw0 = b.op("tensor", lambda t: t.transpose(pb[0][:, 0:32], urow[:, :],
                                                           ident[:32, :32]),
                           deps=ps.wdeps(0) + [h_ur, ld0[0]])
                ps.wrote(0, hw0)
                hw1 = b.op("tensor", lambda t: t.transpose(pb[0][:1, 32:64], c2[:, 0:1],
                                                           ident[:32, :32]),
                           deps=[h_c2a])
                ps.wrote(0, hw1)
                hw2 = b.op("tensor", lambda t: t.transpose(pb[0][:1, 64:96], c2[:, 1:2],
                                                           ident[:32, :32]),
                           deps=[h_c2b])
                ps.wrote(0, hw2)
                h_wm = b.op("vector", lambda v: v.tensor_copy(wmat[:, :], pb[0][:, 0:32]),
                            deps=ps.rdeps(0) + [free.get("wmat_free")])
                ps.read(0, h_wm)
                h_em = b.op("scalar", lambda s: s.activation(emrow[:, :], pb[0][:1, 32:64],
                                                             AF.Exp),
                            deps=ps.rdeps(0) + [free.get("emrow_free")])
                ps.read(0, h_em)
                h_bb = b.op("scalar", lambda s: s.activation(bbrow[:, :], pb[0][:1, 64:96],
                                                             AF.Exp),
                            deps=ps.rdeps(0) + [free.get("bbrow_free")])
                ps.read(0, h_bb)
                free["c2f"] = h_bb
                hbe = b.op("tensor", lambda t: t.matmul(pb[0][:, 96:128], ones_row[:, :],
                                                        emrow[:, :], start=True, stop=True),
                           deps=ps.wdeps(0) + [h_em, h_onesrow])
                ps.wrote(0, hbe)
                hbb = b.op("tensor", lambda t: t.matmul(pb[0][:, 128:160], ones_row[:, :],
                                                        bbrow[:, :], start=True, stop=True),
                           deps=[h_bb])
                ps.wrote(0, hbb)
                h_emat = b.op("vector", lambda v: v.tensor_copy(emat[:, :], pb[0][:, 96:128]),
                              deps=ps.rdeps(0) + [free.get("emat_free")])
                ps.read(0, h_emat)
                h_bbm = b.op("vector", lambda v: v.tensor_copy(bbmat[:, :], pb[0][:, 128:160]),
                             deps=ps.rdeps(0) + [free.get("bbmat_free")])
                ps.read(0, h_bbm)
                free["emrow_free"] = hbe
                free["bbrow_free"] = hbb
                free["urow_free"] = hw0
                h_sm = []
                if hs_dS["valid"]:
                    for bb2 in range(4):
                        hsm = b.op("vector", lambda v, _b=bb2: v.tensor_tensor(
                            smat[:, 512 * _b:512 * (_b + 1)],
                            pb[4 + _b][:, :],
                            fb(emat, 8 * _b, [[1, 8], [0, 64]]), OP.mult),
                            deps=ps.rdeps(4 + bb2) + [h_emat, free.get("smat_free")])
                        ps.read(4 + bb2, hsm)
                        h_sm.append(hsm)
                else:
                    h_sm = [h_smat0]
                h_xpp = []
                for ggrp in range(4):
                    bank = 1 + (ggrp % 2)
                    hts = None
                    dport = ps.wdeps(bank)
                    for jj in range(4):
                        j = 4 * ggrp + jj
                        hts = b.op("tensor", lambda t, _j=j, _jj=jj, _b=bank, _q=qs:
                                   t.transpose(pb[_b][:, 64 * _jj:64 * (_jj + 1)].bitcast(BF16),
                                               xbc[:, _j, _q:_q + 128], identb),
                                   deps=(dport + [free[f"xbc{j}"], h_identb]) if jj == 0
                                   else [free[f"xbc{j}"]],
                                   inc=(jj == 3))
                        ps.wrote(bank, hts)
                    hev = b.op("vector", lambda v, _g=ggrp, _b=bank: v.tensor_tensor(
                        xpp[:, 512 * _g:512 * (_g + 1)].rearrange("p (a b c) -> p a b c",
                                                                  a=4, b=2),
                        pb[_b][:, 0:256].bitcast(BF16).rearrange("p (a b c) -> p a b c",
                                                                 a=4, b=2),
                        fb(wmat, 8 * _g, [[2, 4], [1, 2], [0, 64]]), OP.mult),
                        deps=ps.rdeps(bank) + [h_wm, free.get("xpp_free")])
                    ps.read(bank, hev)
                    h_xpp.append(hev)
                h_xppb = b.op("vector", lambda v: v.tensor_tensor(
                    xppb[:, :].rearrange("p (a b) -> p a b", a=32),
                    xpp[:, :].rearrange("p (a b) -> p a b", a=32),
                    fb(bbmat, 0, [[1, 32], [0, 64]]), OP.mult),
                    deps=h_xpp + [h_bbm, free.get("xppb_free")])
                hbt = b.op("tensor", lambda t, _q=qs: t.transpose(
                    pb[3][:, 0:64].bitcast(BF16), xbc[:, 16, _q:_q + 128], identb),
                    deps=ps.wdeps(3) + [free["xbc16"], h_identb])
                ps.wrote(3, hbt)
                hg = b.op("tensor", lambda t, _q=qs: t.matmul(
                    pb[3][:, 128:256], xbc[:, 16, _q:_q + 128], xbc[:, 17, _q:_q + 128],
                    start=True, stop=True), deps=[free["xbc17"]])
                ps.wrote(3, hg)
                h_btm = b.op("vector", lambda v: v.tensor_copy(btm[:, :],
                                                               pb[3][:, 0:64].bitcast(BF16)),
                             deps=ps.rdeps(3) + [free.get("btm_free")])
                ps.read(3, h_btm)
                h_gm = b.op("vector", lambda v: v.tensor_tensor(
                    gmref[:, :], pb[3][:, 128:256], trimask[:, :], OP.mult),
                    deps=ps.rdeps(3) + [ld0[1], free.get("gm_free")])
                ps.read(3, h_gm)
                h_vf = []
                for rr in range(4):
                    bank = 1 + (rr % 2)
                    hv = None
                    dport = ps.wdeps(bank)
                    for jj in range(4):
                        j = 4 * rr + jj
                        hv = b.op("tensor", lambda t, _j=j, _jj=jj, _b=bank: t.matmul(
                            pb[_b][:, 128 * _jj:128 * (_jj + 1)], sel[:, _j, :], vbf[:, :],
                            start=True, stop=True),
                            deps=(dport + [h_vbf, ld0[2]]) if jj == 0 else [],
                            inc=(jj == 3))
                        ps.wrote(bank, hv)
                    hve = b.op("vector", lambda v, _r=rr, _b=bank: v.tensor_copy(
                        vfull[:, 4 * _r:4 * (_r + 1), :],
                        pb[_b][:, :].rearrange("p (a b) -> p a b", a=4)),
                        deps=ps.rdeps(bank) + [free.get("vfull_free")])
                    ps.read(bank, hve)
                    h_vf.append(hve)
                free["vbf_free"] = h_vf[3]
                h_ymm = [None] * 4
                for hp in range(16):
                    bank = hp // 4
                    reg = (hp % 4) * 128
                    first_in_bank = (hp % 4 == 0)
                    dmm = (ps.wdeps(bank) if first_in_bank else []) + \
                          [h_xpp[hp // 4], h_gm]
                    hin = b.op("tensor", lambda t, _hp=hp, _b=bank, _r=reg, _gq=gq: t.matmul(
                        pb[_b][:, _r:_r + 128], xpp[:, 128 * _hp:128 * (_hp + 1)],
                        gmref[:, :], start=True, stop=(_gq == 0)),
                        deps=dmm, inc=(gq == 0))
                    if gq > 0:
                        hin = b.op("tensor", lambda t, _hp=hp, _b=bank, _r=reg, _q=qs:
                                   t.matmul(pb[_b][:, _r:_r + 128],
                                            smat[:, 128 * _hp:128 * (_hp + 1)],
                                            xbc[:, 17, _q:_q + 128],
                                            start=False, stop=True),
                                   deps=h_sm + [free["xbc17"]])
                    ps.wrote(bank, hin)
                    h_ymm[bank] = hin
                for bb2 in range(4):
                    hds = b.op("tensor", lambda t, _b=bb2: t.matmul(
                        pb[4 + _b][:, :], btm[:, :], xppb[:, 512 * _b:512 * (_b + 1)],
                        start=True, stop=True),
                        deps=ps.wdeps(4 + bb2) + [h_btm, h_xppb])
                    ps.wrote(4 + bb2, hds)
                hs_dS["valid"] = True
                h_ya = []
                for bank in range(4):
                    hy = b.op("vector", lambda v, _b=bank: v.tensor_tensor(
                        yscr[:, 512 * _b:512 * (_b + 1)].rearrange("p (a b) -> p a b", a=4),
                        pb[_b][:, :].rearrange("p (a b) -> p a b", a=4),
                        vfull[:, 4 * _b:4 * (_b + 1), :], OP.mult),
                        deps=ps.rdeps(bank) + [h_vf[bank], free.get("yscr_free")])
                    ps.read(bank, hy)
                    h_ya.append(hy)
                ds_handles = [ps.writer[4 + i] for i in range(4)]
                h_dxp = b.op("vector", lambda v, _q=qs: v.tensor_tensor(
                    xppb[:, :].rearrange("p (a b) -> p a b", a=16),
                    xbc[:, 0:16, _q:_q + 128],
                    bass.AP(drep, 0, [[16, 128], [1, 16], [0, 128]]), OP.mult),
                    deps=ds_handles + [ld0[6]])
                h_y = b.op("vector", lambda v: v.tensor_tensor(
                    yscr[:, :], yscr[:, :], xppb[:, :], OP.add),
                    deps=h_ya + [h_dxp])
                free["xppb_free"] = h_y
                h_g = b.op("vector", lambda v, _q=qs: v.tensor_tensor(
                    g_sb[:, :, _q:_q + 128],
                    yscr[:, :].rearrange("p (a b) -> p a b", a=16),
                    sz[:, :, _q:_q + 128], OP.mult),
                    deps=[h_y, free.get("g_free")])
                free["yscr_free"] = h_g
                free["xpp_free"] = h_ymm
                free["wmat_free"] = h_xpp[3]
                free["emat_free"] = h_sm[-1]
                free["bbmat_free"] = h_xppb
                free["btm_free"] = ds_handles
                free["gm_free"] = h_ymm[3]
                free["vfull_free"] = h_ya
                free["smat_free"] = h_ymm[3]
                if ql == 3:
                    free["dtrow"] = h_ur
                    for j in range(18):
                        free[f"xbcf{j}"] = [h_dxp, h_ymm[3]]
                    for mi2 in range(16):
                        free[f"sz{mi2}"] = h_g

            if tci == 0 and tap_d:
                def emit_tap(name, src_ap, deps):
                    if name not in tap_d:
                        return None
                    hh = b.dma("gpsimd", tap_d[name].ap(), src_ap, deps=deps)
                    free["tapblock"] = (free.get("tapblock") or []) + [hh]
                    return hh
                emit_tap("ln1", ln1, [free.get("ln1f"), free["dt_ready"]])
                emit_tap("sz", sz, sz_done)
                emit_tap("dt", dtrow[:, :], [free["dt_ready"], hal])
                emit_tap("c", crow[:, :], c_ready)
                emit_tap("xbc", xbc, [free.get("xbc17")])
                emit_tap("g", g_sb, [free.get("g_free"), free["yscr_free"]])
                if free.get("tapblock"):
                    tb = free["tapblock"]
                    free["ln1f"] = Bld._flat([free.get("ln1f")]) + tb
                    for j in range(18):
                        free[f"xbcf{j}"] = (free.get(f"xbcf{j}") or []) + tb
                    for m2 in range(16):
                        free[f"sz{m2}"] = Bld._flat([free.get(f"sz{m2}")]) + tb
                    free["g_free"] = Bld._flat([free.get("g_free")]) + tb

            yv = yscr[:, :].rearrange("p (a b) -> p a b", a=4)
            hsq = None
            for gq4 in range(4):
                hq = b.op("scalar", lambda s, _g=gq4: s.activation(
                    yv, g_sb[:, 4 * _g:4 * (_g + 1), :], AF.Square),
                    deps=[free.get("gsq4"), free["yscr_free"]])
                for kk in range(4):
                    k = 4 * gq4 + kk
                    dsq = [hq] if kk == 0 else []
                    if k == 0:
                        dsq = dsq + ps.wdeps(0)
                    hsq = b.op("tensor", lambda t, _k=k, _kk=kk: t.matmul(
                        pb[0][:1, :], onescb, yv[:, _kk, :],
                        start=(_k == 0), stop=(_k == 15)),
                        deps=dsq, inc=True)
                free["gsq4"] = hsq
            ps.wrote(0, hsq)
            free["yscr_free"] = hsq
            h1 = b.op("vector", lambda v: v.tensor_scalar(
                row_a[:, :], pb[0][:1, :], 1.0 / DIN, None, OP.mult),
                deps=ps.rdeps(0) + [free.get("rows_free")])
            ps.read(0, h1)
            h2 = b.op("vector", lambda v: v.tensor_scalar_add(row_a[:, :], row_a[:, :], EPS),
                      deps=[h1])
            h3 = b.op("vector", lambda v: v.reciprocal(row_a[:, :], row_a[:, :]), deps=[h2])
            h4 = b.op("scalar", lambda s: s.activation(row_a[:, :], row_a[:, :], AF.Sqrt),
                      deps=[h3])
            hrb = emit_bcast_row(row_a[:, :], 1, [h4])
            free["rows_free"] = [hrb]
            gn_done = []
            for k in range(16):
                hgn = b.op("vector", lambda v, _k=k: v.scalar_tensor_tensor(
                    g_sb[:, _k, :], g_sb[:, _k, :], normw[:, _k:_k + 1], pb[1][:, :],
                    OP.mult, OP.mult), deps=ps.rdeps(1) + [ld0[5], free[f"sq{k % 2}"]])
                ps.read(1, hgn)
                gn_done.append(hgn)
            for mi in range(8):
                gmi = tci * 8 + mi
                hwl = b.dma("gpsimd", fview(woutbuf[gmi % 2][:, :, :], BF16), wout_src(mi),
                            deps=[free.get(f"woutb{gmi % 2}")])
                bank = 2 + (mi % 2)
                hmm = None
                dmm = ps.wdeps(bank) + [hwl] + gn_done
                for k in range(16):
                    hmm = b.op("tensor", lambda t, _mi=mi, _k=k, _b=bank, _g=gmi: t.matmul(
                        pb[_b][:, :], fview(woutbuf[_g % 2][:, _k, :], BF16), g_sb[:, _k, :],
                        start=(_k == 0), stop=(_k == 15)),
                        deps=dmm if k == 0 else [], inc=(k == 15))
                ps.wrote(bank, hmm)
                free[f"woutb{gmi % 2}"] = hmm
                free["g_free"] = hmm
                hx1 = b.op("vector", lambda v, _mi=mi, _b=bank, _t0=t0: v.tensor_tensor(
                    x_fm[:, _mi, _t0:_t0 + TC], x_fm[:, _mi, _t0:_t0 + TC], pb[_b][:, :],
                    OP.add), deps=ps.rdeps(bank))
                ps.read(bank, hx1)
                free[f"x1w{tci}_{mi}"] = hx1

        if PHASE_B:
            BAR = [(en, b.cnt[en]) for en in ENGS if b.cnt[en] > 0]
            TB = 1024
            NTB = T // TB
            xf = ab.view(o_sz, [128, NKC, T])
            hbit = x_fm[:, :, :].bitcast(BF16).rearrange("p a (b c) -> p (a b) c", b=4)
            x1buf = x_fm[:, 4:8, :]
            absscr = ab.view(o_g, [128, NKC, TC])
            sq2scr = ab.view(o_g + 4096, [128, NKC, TC])
            w1buf = [ab.view(o_g + i * 2048, [128, NKC, 256]) for i in range(4)]
            acc = ab.view(o_wout, [128, NKC, TB])
            tscr = ab.view(o_sqscr, [128, TB])
            wbc = ab.view(o_cacc, [128, TB])
            sel8_sb = ab.alloc([8, 8, 128])
            gw3_sb = ab.view(o_vbf, [128, 4, 8])
            _af_saved = af.off
            af.off = 0
            cmpscr = af.alloc([128, 128])
            eb = {}
            eb["e0b1"] = af.alloc([128, 3, 32]); eb["e0b2"] = af.alloc([128, 3, 8])
            eb["e1b1"] = af.alloc([128, 3, 16]); eb["e1b2"] = af.alloc([128, 3, 16])
            eb["e1b3"] = af.alloc([128, 3, 8]); eb["e2b1"] = af.alloc([128, 2, 24])
            eb["e2b2"] = af.alloc([128, 2, 8])
            affw_sb = af.alloc([128, NKC, 8])
            assert af.off <= 1440, af.off
            af.off = _af_saved
            hld = [b.dma("sync", sel8_sb, bass.AP(d_sel8, 0, [[8 * 128, 8], [128, 8], [1, 128]]), deps=BAR),
                   b.dma("gpsimd", gw3_sb, bass.AP(d_gw3, 0, [[8, 128], [128 * 8, 4], [1, 8]]), deps=BAR),
                   b.dma("sync", eb["e0b1"], d_e0b1[:, :, :], deps=BAR),
                   b.dma("sync", eb["e0b2"], d_e0b2[:, :, :], deps=BAR),
                   b.dma("sync", eb["e1b1"], d_e1b1[:, :, :], deps=BAR),
                   b.dma("sync", eb["e1b2"], d_e1b2[:, :, :], deps=BAR),
                   b.dma("sync", eb["e1b3"], d_e1b3[:, :, :], deps=BAR),
                   b.dma("sync", eb["e2b1"], d_e2b1[:, :, :], deps=BAR),
                   b.dma("sync", eb["e2b2"], d_e2b2[:, :, :], deps=BAR)]
            h_sel8 = hld[0]

            imp_dma = []
            l1_dma = []
            for tci in range(NTC):
                t0 = tci * TC
                bs_s = 0 if tci % 2 == 0 else 2
                bq_s = bs_s + 1
                hsumx = None
                for k in range(NKC):
                    hsumx = b.op("tensor", lambda t, _k=k, _t0=t0, _b=bs_s: t.matmul(
                        pb[_b][:1, :], ones_col[:, :], x_fm[:, _k, _t0:_t0 + TC],
                        start=(_k == 0), stop=(_k == NKC - 1)),
                        deps=(ps.wdeps(bs_s) + BAR + [h_ones]) if k == 0 else [], inc=True)
                ps.wrote(bs_s, hsumx)
                hsq2 = b.op("scalar", lambda s, _t0=t0, _p=tci % 2: s.activation(
                    sq2scr[:, :, :] if _p == 0 else absscr[:, :, :],
                    x_fm[:, :, _t0:_t0 + TC], AF.Square),
                    deps=BAR + [free.get(f"sq2free{tci % 2}"), free.get("absfree")])
                hsumq = None
                for k in range(NKC):
                    hsumq = b.op("tensor", lambda t, _k=k, _b=bq_s, _p=tci % 2: t.matmul(
                        pb[_b][:1, :], onescb,
                        (sq2scr if _p == 0 else absscr)[:, _k, :],
                        start=(_k == 0), stop=(_k == NKC - 1)),
                        deps=(ps.wdeps(bq_s) + [hsq2, h_onescb]) if k == 0 else [], inc=True)
                ps.wrote(bq_s, hsumq)
                free[f"sq2free{tci % 2}"] = hsumq
                h1 = b.op("vector", lambda v, _b=bs_s: v.tensor_scalar(
                    row_a[:, :], pb[_b][:1, :], LNW, None, OP.mult),
                    deps=ps.rdeps(bs_s) + [free.get("rows_free")])
                ps.read(bs_s, h1)
                h2 = b.op("vector", lambda v: v.tensor_mul(row_b[:, :], row_a[:, :], row_a[:, :]),
                          deps=[h1])
                h3 = b.op("vector", lambda v, _b=bq_s: v.scalar_tensor_tensor(
                    row_b[:, :], pb[_b][:1, :], LNW, row_b[:, :], OP.mult, OP.subtract),
                    deps=ps.rdeps(bq_s) + [h2])
                ps.read(bq_s, h3)
                hi1 = b.op("vector", lambda v: v.tensor_scalar_add(row_c[:, :], row_b[:, :], EPS),
                           deps=[h3, free.get("rows_free")])
                hi2 = b.op("vector", lambda v: v.reciprocal(row_c[:, :], row_c[:, :]), deps=[hi1])
                hi3 = b.op("vector", lambda v: v.tensor_mul(row_c[:, :], row_c[:, :], row_b[:, :]),
                           deps=[hi2])
                hi4 = b.op("vector", lambda v: v.tensor_scalar(row_c[:, :], row_c[:, :], float(DM),
                                                               None, OP.mult), deps=[hi3])
                hi5 = b.op("scalar", lambda s: s.activation(row_c[:, :], row_c[:, :], AF.Sqrt),
                           deps=[hi4])
                imp_dma.append(b.dma("sync", d_imp_in[0:1, t0:t0 + TC], row_c[:, :], deps=[hi5]))
                h4 = b.op("vector", lambda v: v.tensor_scalar_add(row_b[:, :], row_b[:, :], EPS),
                          deps=[h3, hi3])
                h5 = b.op("vector", lambda v: v.reciprocal(row_b[:, :], row_b[:, :]), deps=[h4])
                h6 = b.op("scalar", lambda s: s.activation(row_b[:, :], row_b[:, :], AF.Sqrt),
                          deps=[h5])
                h7 = b.op("vector", lambda v: v.tensor_scalar(
                    row_c[:, :], row_a[:, :], -1.0, None, OP.mult), deps=[h1, imp_dma[-1]])
                h8 = b.op("vector", lambda v: v.tensor_mul(row_c[:, :], row_c[:, :], row_b[:, :]),
                          deps=[h6, h7])
                hr = emit_bcast_row(row_b[:, :], bs_s, [h6])
                hn = emit_bcast_row(row_c[:, :], bq_s, [h8])
                free["rows_free"] = [hr, hn]
                ln2_done = []
                for k in range(NKC):
                    ha = b.op("vector", lambda v, _k=k, _t0=t0, _b=bs_s: v.tensor_tensor(
                        xf[:, _k, _t0:_t0 + TC], x_fm[:, _k, _t0:_t0 + TC], pb[_b][:, :],
                        OP.mult), deps=ps.rdeps(bs_s) + BAR)
                    ps.read(bs_s, ha)
                    hb2 = b.op("vector", lambda v, _k=k, _t0=t0, _b=bq_s: v.tensor_tensor(
                        xf[:, _k, _t0:_t0 + TC], xf[:, _k, _t0:_t0 + TC], pb[_b][:, :], OP.add),
                        deps=ps.rdeps(bq_s) + [ha])
                    ps.read(bq_s, hb2)
                    ln2_done.append(hb2)
                habs = b.op("scalar", lambda s, _t0=t0: s.activation(
                    absscr[:, :, :], xf[:, :, _t0:_t0 + TC], AF.Abs),
                    deps=ln2_done + [free.get("absfree"), free.get("sq2free1")])
                bl1 = 4 + tci % 2
                hl1 = None
                for k in range(NKC):
                    hl1 = b.op("tensor", lambda t, _k=k, _b=bl1: t.matmul(
                        pb[_b][:1, :], onescb, absscr[:, _k, :],
                        start=(_k == 0), stop=(_k == NKC - 1)),
                        deps=([habs] + ps.wdeps(bl1)) if k == 0 else [], inc=True)
                ps.wrote(bl1, hl1)
                free["absfree"] = hl1
                hl2 = b.op("vector", lambda v, _b=bl1: v.tensor_copy(row_a[:, :], pb[_b][:1, :]),
                           deps=ps.rdeps(bl1))
                ps.read(bl1, hl2)
                l1_dma.append(b.dma("sync", d_l1[0:1, t0:t0 + TC], row_a[:, :], deps=[hl2]))
                free["rows_free"] = free["rows_free"] + [hl2, l1_dma[-1]]

            h_spill = b.dma("sync", d_x1[:, :],
                            x_fm[:, :, :].rearrange("p a b -> p (a b)"),
                            deps=BAR + imp_dma + l1_dma + free["rows_free"])
            for m in range(32):
                free[f"hex{m}"] = [h_spill]
            hcc = b.op("gpsimd", lambda g: g.collective_compute(
                "AllGather", OP.bypass, replica_groups=[list(range(NCORES))],
                ins=[d_imp_in.ap().opt()], outs=[d_imp_out.ap().opt()]),
                deps=imp_dma)
            NR = (NCORES * T) // 128
            himp = b.dma("sync", impall[:, 0:NR],
                         bass.AP(d_imp_out, 0, [[NR, 128], [1, NR]]), deps=[hcc])

            NIMP = NCORES * T
            hb0m = b.op("vector", lambda v: v.memset(bis[:, 0:2], 0.0))
            hb1m = b.op("vector", lambda v: v.memset(bis[:, 2:4], 64.0), deps=[hb0m])
            hkt = ld0[16]
            hlast = [hb1m, himp, hkt]
            for it in range(45):
                ht1 = b.op("vector", lambda v: v.tensor_add(bis[:, 4:6], bis[:, 0:2], bis[:, 2:4]),
                           deps=hlast)
                ht2 = b.op("vector", lambda v: v.tensor_scalar(bis[:, 4:6], bis[:, 4:6], 0.5,
                                                               None, OP.mult), deps=[ht1])
                hcnt = []
                for j in range(2):
                    hc1 = b.op("vector", lambda v, _j=j: v.tensor_scalar(
                        cmpscr[:, 0:NR], impall[:, 0:NR], bis[:, 4 + _j:5 + _j], None,
                        OP.is_le), deps=[ht2, himp])
                    hc2 = b.op("vector", lambda v, _j=j: v.tensor_reduce(
                        bis[:, 6 + _j:7 + _j], cmpscr[:, 0:NR], AX.X, OP.add), deps=[hc1])
                    hcnt.append(hc2)
                hmm2 = b.op("tensor", lambda t: t.matmul(
                    pb[0][:, 0:2], ones_sq[:, :], bis[:, 6:8], start=True, stop=True),
                    deps=ps.wdeps(0) + hcnt + [h_onesq])
                ps.wrote(0, hmm2)
                hpred = b.op("vector", lambda v: v.tensor_tensor(
                    bis[:, 8:10], pb[0][:, 0:2], ktarg[:, :], OP.is_ge),
                    deps=ps.rdeps(0) + [hkt])
                ps.read(0, hpred)
                hd1 = b.op("vector", lambda v: v.tensor_sub(bis[:, 10:12], bis[:, 4:6], bis[:, 2:4]),
                           deps=[hpred])
                hd2 = b.op("vector", lambda v: v.tensor_mul(bis[:, 10:12], bis[:, 10:12],
                                                            bis[:, 8:10]), deps=[hd1])
                hd3 = b.op("vector", lambda v: v.tensor_add(bis[:, 2:4], bis[:, 2:4],
                                                            bis[:, 10:12]), deps=[hd2])
                hn1 = b.op("vector", lambda v: v.tensor_scalar(bis[:, 12:14], bis[:, 8:10], -1.0,
                                                               1.0, OP.mult, OP.add), deps=[hpred])
                hn2 = b.op("vector", lambda v: v.tensor_sub(bis[:, 10:12], bis[:, 4:6],
                                                            bis[:, 0:2]), deps=[hd2])
                hn3 = b.op("vector", lambda v: v.tensor_mul(bis[:, 10:12], bis[:, 10:12],
                                                            bis[:, 12:14]), deps=[hn1, hn2])
                hn4 = b.op("vector", lambda v: v.tensor_add(bis[:, 0:2], bis[:, 0:2],
                                                            bis[:, 10:12]), deps=[hn3])
                hlast = [hd3, hn4]
            hth1 = b.op("vector", lambda v: v.tensor_scalar(
                bis[:, 14:15], bis[:, 2:3], float(_PCT[0]), None, OP.mult), deps=hlast)
            hthr = b.op("vector", lambda v: v.scalar_tensor_tensor(
                bis[:, 14:15], bis[:, 3:4], float(_PCT[1]), bis[:, 14:15], OP.mult, OP.add),
                deps=[hth1])
            hs1 = b.op("vector", lambda v: v.tensor_reduce(bis[:, 6:7], impall[:, 0:NR],
                                                           AX.X, OP.add),
                       deps=[himp] + hlast)
            hs2 = b.op("tensor", lambda t: t.matmul(pb[0][:, 0:1], ones_sq[:, :], bis[:, 6:7],
                                                    start=True, stop=True),
                       deps=ps.wdeps(0) + [hs1])
            ps.wrote(0, hs2)
            hs3 = b.op("vector", lambda v: v.tensor_scalar(bis[:, 6:7], pb[0][:, 0:1],
                                                           1.0 / NIMP, None, OP.mult),
                       deps=ps.rdeps(0))
            ps.read(0, hs3)
            hs4 = b.op("vector", lambda v: v.tensor_scalar(cmpscr[:, 0:NR], impall[:, 0:NR],
                                                           bis[:, 6:7], None, OP.subtract),
                       deps=[hs3])
            hs5 = b.op("vector", lambda v: v.tensor_mul(cmpscr[:, 0:NR], cmpscr[:, 0:NR],
                                                        cmpscr[:, 0:NR]), deps=[hs4])
            hs6 = b.op("vector", lambda v: v.tensor_reduce(bis[:, 6:7], cmpscr[:, 0:NR],
                                                           AX.X, OP.add), deps=[hs5])
            hs7 = b.op("tensor", lambda t: t.matmul(pb[0][:, 0:1], ones_sq[:, :], bis[:, 6:7],
                                                    start=True, stop=True),
                       deps=ps.wdeps(0) + [hs6])
            ps.wrote(0, hs7)
            hs8 = b.op("vector", lambda v: v.tensor_scalar(bis[:, 15:16], pb[0][:, 0:1],
                                                           1.0 / NIMP, None, OP.mult),
                       deps=ps.rdeps(0))
            ps.read(0, hs8)
            hs9 = b.op("scalar", lambda s: s.activation(bis[:, 15:16], bis[:, 15:16], AF.Sqrt),
                       deps=[hs8])
            htmp = b.op("vector", lambda v: v.tensor_scalar(bis[:, 15:16], bis[:, 15:16], 0.1,
                                                            1.0, OP.mult, OP.add), deps=[hs9])

            xf_done = []
            for tci in range(NTC):
                t0 = tci * TC
                hia = b.dma("sync", row_a[:, :], d_imp_in[0:1, t0:t0 + TC],
                            deps=[hthr, imp_dma[tci], free.get("rows_free")])
                hib = b.dma("sync", row_b[:, :], d_l1[0:1, t0:t0 + TC],
                            deps=[htmp, l1_dma[tci], free.get("rows_free")])
                hm1 = b.op("vector", lambda v: v.tensor_scalar(
                    row_c[:, :], row_a[:, :], bis[0:1, 14:15], None, OP.is_gt),
                    deps=[hia, hthr, free.get("rows_free")])
                hm2 = b.op("vector", lambda v: v.tensor_scalar_add(row_b[:, :], row_b[:, :], 1e-8),
                           deps=[hib])
                hm3 = b.op("vector", lambda v: v.tensor_scalar(
                    row_b[:, :], row_b[:, :], bis[0:1, 15:16], None, OP.mult),
                    deps=[hm2, htmp])
                hm4 = b.op("vector", lambda v: v.reciprocal(row_b[:, :], row_b[:, :]), deps=[hm3])
                hm5 = b.op("vector", lambda v: v.tensor_mul(row_c[:, :], row_c[:, :], row_b[:, :]),
                           deps=[hm1, hm4])
                hsb = emit_bcast_row(row_c[:, :], 0, [hm5])
                free["rows_free"] = [hsb]
                for k in range(NKC):
                    hx = b.op("vector", lambda v, _k=k, _t0=t0: v.tensor_tensor(
                        xf[:, _k, _t0:_t0 + TC], xf[:, _k, _t0:_t0 + TC], pb[0][:, :], OP.mult),
                        deps=ps.rdeps(0))
                    ps.read(0, hx)
                    xf_done.append(hx)

            hxm = []
            for k in range(NKC):
                hxm.append(b.op("vector", lambda v, _k=k: v.tensor_reduce(
                    bis[:, 8 + _k:9 + _k], xf[:, _k, :], AX.X, OP.add), deps=xf_done))
            haw = b.dma("sync", affw_sb, bass.AP(d_affw, 0, [[8, 128], [128 * 8, NKC], [1, 8]]),
                        deps=BAR)
            haf = None
            for k in range(NKC):
                haf = b.op("tensor", lambda t, _k=k: t.matmul(
                    pb[1][:8, 0:1], affw_sb[:, _k, :], bis[:, 8 + _k:9 + _k],
                    start=(_k == 0), stop=(_k == NKC - 1)),
                    deps=(ps.wdeps(1) + [haw] + hxm) if k == 0 else [], inc=True)
            ps.wrote(1, haf)
            ha2 = b.op("vector", lambda v: v.scalar_tensor_tensor(
                affb[:, :], pb[1][:8, 0:1], 1.0 / T, affb[:, :], OP.mult, OP.add),
                deps=ps.rdeps(1) + [ld0[14]])
            ps.read(1, ha2)
            ha3 = b.op("vector", lambda v: v.tensor_scalar(affb[:, :], affb[:, :], 0.3,
                                                           None, OP.mult), deps=[ha2])
            ha4 = b.op("tensor", lambda t: t.transpose(pb[1][:1, 0:8], affb[:, :],
                                                       ident[:8, :8]),
                       deps=ps.wdeps(1) + [ha3, ld0[0]])
            ps.wrote(1, ha4)
            ha5 = b.op("vector", lambda v: v.tensor_copy(scl_sb[0:1, 0:8], pb[1][:1, 0:8]),
                       deps=ps.rdeps(1))
            ps.read(1, ha5)
            ha6 = b.op("tensor", lambda t: t.matmul(pb[1][:, 0:8], ones_row[:, :],
                                                    scl_sb[0:1, 0:8], start=True, stop=True),
                       deps=ps.wdeps(1) + [ha5, h_onesrow])
            ps.wrote(1, ha6)
            ha7 = b.op("vector", lambda v: v.tensor_copy(affbc[:, :], pb[1][:, 0:8]),
                       deps=ps.rdeps(1))
            ps.read(1, ha7)

            he0 = b.op("vector", lambda v: v.tensor_scalar_add(nsc[:8, 0:1], eu_sb[:, :], 1e-8),
                       deps=[ld0[15]])
            he1 = b.op("scalar", lambda s: s.activation(nsc[:8, 0:1], nsc[:8, 0:1], AF.Ln),
                       deps=[he0])
            he2 = b.op("vector", lambda v: v.tensor_mul(nsc[:8, 0:1], nsc[:8, 0:1], eu_sb[:, :]),
                       deps=[he1])
            he3 = b.op("tensor", lambda t: t.matmul(pb[2][:1, 0:1], ones_col[:8, :],
                                                    nsc[:8, 0:1], start=True, stop=True),
                       deps=ps.wdeps(2) + [he2, h_ones])
            ps.wrote(2, he3)
            he4 = b.op("vector", lambda v: v.tensor_scalar(scl_sb[0:1, 1:2], pb[2][:1, 0:1],
                                                           0.01, 0.02, OP.mult, OP.add),
                       deps=ps.rdeps(2))
            ps.read(2, he4)
            he5 = b.op("vector", lambda v: v.tensor_scalar(scl_sb[0:1, 1:2], scl_sb[0:1, 1:2],
                                                           1e-4, None, OP.max), deps=[he4])
            he6 = b.op("tensor", lambda t: t.matmul(pb[2][:, 0:1], ones_row[:, :],
                                                    scl_sb[0:1, 1:2], start=True, stop=True),
                       deps=ps.wdeps(2) + [he5, h_onesrow])
            ps.wrote(2, he6)
            he7 = b.op("vector", lambda v: v.tensor_copy(nsc[:, :], pb[2][:, 0:1]),
                       deps=ps.rdeps(2))
            ps.read(2, he7)

            def stream_pass(wsrc, nm, nkg, rhs_slices, evac, extra_dep):
                li = [0]
                out_h = []
                for mp in range(nm // 2):
                    hload = {}
                    hmm_last = {0: None, 1: None}
                    for kg in range(nkg):
                        par = li[0] % 4
                        hload[kg] = b.dma("gpsimd", w1buf[par], wsrc(kg, mp),
                                          deps=[free.get(f"w1b{par}")] + list(extra_dep))
                        for mi2 in (0, 1):
                            mi = 2 * mp + mi2
                            bank = [2, 3, 4, 5][mi % 4]
                            hmm = None
                            for k in range(8):
                                kk = kg * 8 + k
                                first = (kg == 0 and k == 0)
                                last = (kg == nkg - 1 and k == 7)
                                dmm = []
                                if first:
                                    dmm = ps.wdeps(bank) + list(extra_dep)
                                if k == 0:
                                    dmm = dmm + [hload[kg]]
                                hmm = b.op("tensor",
                                           lambda t, _p=par, _k=k, _m=mi2, _b=bank,
                                                  _r=rhs_slices[kk], _f=first, _l=last:
                                           t.matmul(pb[_b][:, :],
                                                    w1buf[_p][:, _k, 128 * _m:128 * (_m + 1)],
                                                    _r, start=_f, stop=_l),
                                           deps=dmm, inc=(k == 7))
                            hmm_last[mi2] = hmm
                        free[f"w1b{par}"] = [hmm_last[0], hmm_last[1]]
                        li[0] += 1
                    for mi2 in (0, 1):
                        mi = 2 * mp + mi2
                        bank = [2, 3, 4, 5][mi % 4]
                        ps.wrote(bank, hmm_last[mi2])
                        out_h.append(evac(mi, bank, hmm_last[mi2]))
                return out_h

            def stream_pass(wsrc, nm, nkg, rhs_fn, evac, extra_dep):
                li = [0]
                out_h = []
                for mp in range(nm // 2):
                    hload = {}
                    hlast = {}
                    for kg in range(nkg):
                        par = li[0] % 4
                        hload[kg] = b.dma("gpsimd", w1buf[par], wsrc(kg, mp),
                                          deps=[free.get(f"w1b{par}")] + list(extra_dep))
                        for mi2 in (0, 1):
                            mi = 2 * mp + mi2
                            pair = (2, 3) if mi % 2 == 0 else (4, 5)
                            for half in (0, 1):
                                bank = pair[half]
                                hmm = None
                                for k in range(8):
                                    kk = kg * 8 + k
                                    first = (kg == 0 and k == 0)
                                    last = (kg == nkg - 1 and k == 7)
                                    dmm = []
                                    if first:
                                        dmm = ps.wdeps(bank) + list(extra_dep)
                                    if k == 0:
                                        dmm = dmm + [hload[kg]]
                                    hmm = b.op("tensor",
                                               lambda t, _p=par, _k=k, _m=mi2, _b=bank,
                                                      _r=rhs_fn(kk, half), _f=first, _l=last:
                                               t.matmul(pb[_b][:, :],
                                                        w1buf[_p][:, _k, 128 * _m:128 * (_m + 1)],
                                                        _r, start=_f, stop=_l),
                                               deps=dmm, inc=(k == 7))
                                hlast[(mi2, half)] = hmm
                        free[f"w1b{par}"] = [hlast[(0, 0)], hlast[(0, 1)],
                                             hlast[(1, 0)], hlast[(1, 1)]]
                        li[0] += 1
                    for mi2 in (0, 1):
                        mi = 2 * mp + mi2
                        pair = (2, 3) if mi % 2 == 0 else (4, 5)
                        ps.wrote(pair[0], hlast[(mi2, 0)])
                        ps.wrote(pair[1], hlast[(mi2, 1)])
                        out_h.extend(evac(mi, pair, (hlast[(mi2, 0)], hlast[(mi2, 1)])))
                return out_h

            PHYS = [("e0", 0), ("e0", 1), ("e0", 2),
                    ("e1", 0), ("e1", 1), ("e1", 2),
                    ("e2", 0), ("e2", 1)]
            SLOT = [0, 3, 6, 1, 4, 7, 2, 5]
            out_dmas = []
            gsq = ab.view(o_wout, [128, 8, TC])
            for tcb in range(NTB):
                t0 = tcb * TB
                xf_rhs = lambda kk, half, _t0=t0: xf[:, kk, _t0 + 512 * half:_t0 + 512 * (half + 1)]

                def ev_gateh(mi, pair, hl):
                    hs2 = []
                    for half in (0, 1):
                        hv = b.op("scalar", lambda s, _m=mi, _b=pair[half], _h=half:
                                  s.activation(hbit[:, _m, 512 * _h:512 * (_h + 1)],
                                               pb[_b][:, :], AF.Silu,
                                               bias=gb1[:, _m:_m + 1]),
                                  deps=ps.rdeps(pair[half]) + [ld0[11],
                                                               free.get(f"hex{mi}")])
                        ps.read(pair[half], hv)
                        hs2.append(hv)
                    free[f"hex{mi}"] = hs2
                    return hs2
                gh = stream_pass(
                    lambda kg, mp: bass.AP(d_gw1, 256 * mp,
                                           [[DM, 128], [128 * DM, 8], [1, 256]]),
                    8, 1, xf_rhs, ev_gateh, xf_done)
                hnrm = []
                for blk in range(2):
                    bs = 512 * blk
                    hsumh = None
                    for k in range(8):
                        hsumh = b.op("tensor", lambda t, _k=k, _s=bs: t.matmul(
                            pb[0][:1, :], onescb, hbit[:, _k, _s:_s + 512],
                            start=(_k == 0), stop=(_k == 7)),
                            deps=(ps.wdeps(0) + gh + [h_onescb]) if k == 0 else [],
                            inc=True)
                    ps.wrote(0, hsumh)
                    hsqh = b.op("scalar", lambda s, _s=bs: s.activation(
                        gsq[:, :, :], hbit[:, 0:8, _s:_s + 512], AF.Square),
                        deps=gh + [free.get("gsqfree")])
                    hsumq = None
                    for k in range(8):
                        hsumq = b.op("tensor", lambda t, _k=k: t.matmul(
                            pb[1][:1, :], onescb, gsq[:, _k, :],
                            start=(_k == 0), stop=(_k == 7)),
                            deps=(ps.wdeps(1) + [hsqh, h_onescb]) if k == 0 else [],
                            inc=True)
                    ps.wrote(1, hsumq)
                    free["gsqfree"] = hsumq
                    hrows2 = emit_ln_rows(0, 1, DM, EPS)
                    hr2 = emit_bcast_row(row_b[:, :], 0, [hrows2[0]])
                    hn2 = emit_bcast_row(row_c[:, :], 1, [hrows2[1]])
                    free["rows_free"] = [hr2, hn2]
                    for k in range(8):
                        hq1 = b.op("vector", lambda v, _k=k, _s=bs: v.tensor_tensor(
                            hbit[:, _k, _s:_s + 512], hbit[:, _k, _s:_s + 512],
                            pb[0][:, :], OP.mult), deps=ps.rdeps(0) + gh)
                        ps.read(0, hq1)
                        hq2 = b.op("vector", lambda v, _k=k, _s=bs: v.tensor_tensor(
                            hbit[:, _k, _s:_s + 512], hbit[:, _k, _s:_s + 512],
                            pb[1][:, :], OP.add), deps=ps.rdeps(1) + [hq1])
                        ps.read(1, hq2)
                        hq3 = b.op("vector", lambda v, _k=k, _s=bs: v.tensor_scalar(
                            hbit[:, _k, _s:_s + 512], hbit[:, _k, _s:_s + 512],
                            lnsc[:, _k:_k + 1], lnof[:, _k:_k + 1], OP.mult, OP.add),
                            deps=[hq2, ld0[9], ld0[10]])
                        hnrm.append(hq3)
                for k in range(8):
                    free[f"hex{k}"] = hnrm
                hn_rhs = lambda kk, half: hbit[:, kk, 512 * half:512 * (half + 1)]
                def ev_gateh2(mi, pair, hl):
                    hs2 = []
                    for half in (0, 1):
                        hv = b.op("scalar", lambda s, _m=mi, _b=pair[half], _h=half:
                                  s.activation(hbit[:, 8 + _m, 512 * _h:512 * (_h + 1)],
                                               pb[_b][:, :], AF.Silu,
                                               bias=gb2[:, _m:_m + 1]),
                                  deps=ps.rdeps(pair[half]) + [ld0[12],
                                                               free.get(f"hex{8 + mi}")])
                        ps.read(pair[half], hv)
                        hs2.append(hv)
                    free[f"hex{8 + mi}"] = hs2
                    return hs2
                g2 = stream_pass(
                    lambda kg, mp: bass.AP(d_gw2, 256 * mp,
                                           [[512, 128], [128 * 512, 8], [1, 256]]),
                    4, 1, hn_rhs, ev_gateh2, hnrm)
                wrow_h = []
                for tt in range(8):
                    gt = tcb * 8 + tt
                    ts0 = t0 + tt * 128
                    hlg = None
                    for k in range(4):
                        hlg = b.op("tensor", lambda t, _k=k, _s=ts0 - t0: t.matmul(
                            pb[1][:, 0:8], hbit[:, 8 + _k, _s:_s + 128],
                            gw3_sb[:, _k, :], start=(_k == 0), stop=(_k == 3)),
                            deps=(ps.wdeps(1) + g2 + [hld[1]]) if k == 0 else [],
                            inc=(k == 3))
                    ps.wrote(1, hlg)
                    C = cmpscr
                    hv1 = b.op("vector", lambda v: v.tensor_tensor(
                        C[:, 0:8], pb[1][:, 0:8], gb3bc[:, :], OP.add),
                        deps=ps.rdeps(1) + [h_gb3, free.get("cmpfree")])
                    ps.read(1, hv1)
                    hv2 = b.op("vector", lambda v: v.tensor_tensor(
                        C[:, 0:8], C[:, 0:8], affbc[:, :], OP.add), deps=[hv1, ha7])
                    hv3 = b.op("vector", lambda v, _g=gt: v.scalar_tensor_tensor(
                        C[:, 0:8], noise_sb[:, _g, :], nsc[:, :], C[:, 0:8],
                        OP.mult, OP.add), deps=[hv2, he7, ld0[17]])
                    hv4 = b.op("vector", lambda v: v.tensor_scalar(
                        C[:, 0:8], C[:, 0:8], 1.0 / 1.5, None, OP.mult), deps=[hv3])
                    hv5 = b.op("vector", lambda v: v.tensor_reduce(
                        C[:, 16:17], C[:, 0:8], AX.X, OP.max), deps=[hv4])
                    hv6 = b.op("vector", lambda v: v.tensor_scalar(
                        C[:, 24:32], C[:, 0:8], C[:, 16:17], None, OP.is_equal), deps=[hv5])
                    hv7 = b.op("vector", lambda v: v.scalar_tensor_tensor(
                        C[:, 32:40], C[:, 24:32], -1e30, C[:, 0:8], OP.mult, OP.add),
                        deps=[hv6])
                    hv8 = b.op("vector", lambda v: v.tensor_reduce(
                        C[:, 17:18], C[:, 32:40], AX.X, OP.max), deps=[hv7])
                    hv9 = b.op("vector", lambda v: v.tensor_scalar(
                        C[:, 24:32], C[:, 0:8], C[:, 17:18], None, OP.is_ge), deps=[hv8])
                    hva = b.op("vector", lambda v: v.tensor_scalar(
                        C[:, 32:40], C[:, 0:8], C[:, 16:17], None, OP.subtract), deps=[hv9])
                    hvb = b.op("scalar", lambda s: s.activation(
                        C[:, 32:40], C[:, 32:40], AF.Exp), deps=[hva])
                    hvc = b.op("vector", lambda v: v.tensor_mul(
                        C[:, 40:48], C[:, 32:40], C[:, 24:32]), deps=[hvb])
                    hvd = b.op("vector", lambda v: v.tensor_reduce(
                        C[:, 18:19], C[:, 40:48], AX.X, OP.add), deps=[hvc])
                    hve = b.op("vector", lambda v: v.reciprocal(C[:, 19:20], C[:, 18:19]),
                               deps=[hvd])
                    hvf = b.op("vector", lambda v: v.tensor_scalar(
                        C[:, 0:8], C[:, 40:48], C[:, 19:20], None, OP.mult), deps=[hve])
                    hvt = b.op("tensor", lambda t: t.transpose(
                        pb[1][:8, 128:256], C[:, 0:8], ident[:128, :128]),
                        deps=ps.wdeps(1) + [hvf, ld0[0]])
                    ps.wrote(1, hvt)
                    hvw = b.op("vector", lambda v, _g=gt: v.tensor_copy(
                        wrows[:, 128 * _g:128 * (_g + 1)], pb[1][:8, 128:256]),
                        deps=ps.rdeps(1) + [free.get("wrows_free")])
                    ps.read(1, hvw)
                    free["cmpfree"] = hvf
                    wrow_h.append(hvw)

                for p, (et, ei) in enumerate(PHYS):
                    slot = SLOT[p]
                    hwb = []
                    for half in (0, 1):
                        bank = 6 + half
                        h1w = b.op("tensor", lambda t, _s=slot, _t0=t0, _h=half, _b=bank:
                                   t.matmul(pb[_b][:, :], sel8_sb[:, _s, :],
                                            wrows[:, _t0 + 512 * _h:_t0 + 512 * (_h + 1)],
                                            start=True, stop=True),
                                   deps=ps.wdeps(bank) + wrow_h + [h_sel8])
                        ps.wrote(bank, h1w)
                        h2w = b.op("vector", lambda v, _h=half, _b=bank: v.tensor_copy(
                            wbc[:, 512 * _h:512 * (_h + 1)], pb[_b][:, :]),
                            deps=ps.rdeps(bank) + [free.get("wbcfree")])
                        ps.read(bank, h2w)
                        hwb.append(h2w)
                    if et == "e0":
                        nmh, nkg_o, wh, wo, b1t, b2t = 32, 4, d_e0w1, d_e0w2, eb["e0b1"], eb["e0b2"]
                        hsz, act = 4096, AF.Gelu_apprx_tanh
                    elif et == "e1":
                        nmh, nkg_o, wh, wo, b1t, b2t = 16, 2, d_e1w1, d_e1w3, eb["e1b1"], eb["e1b3"]
                        hsz, act = DIN, AF.Silu
                    else:
                        nmh, nkg_o, wh, wo, b1t, b2t = 24, 3, d_e2w1, d_e2w2, eb["e2b1"], eb["e2b2"]
                        hsz, act = 3072, AF.Sigmoid
                    def ev_h(mi, pair, hl, _act=act, _b1=b1t, _ei=ei, _et=et):
                        hs2 = []
                        for half in (0, 1):
                            bank = pair[half]
                            if _et != "e2":
                                hv = b.op("scalar", lambda s, _m=mi, _b=bank, _h=half:
                                          s.activation(hbit[:, _m, 512 * _h:512 * (_h + 1)],
                                                       pb[_b][:, :], _act,
                                                       bias=_b1[:, _ei, _m:_m + 1]),
                                          deps=ps.rdeps(bank) + hld +
                                               [free.get(f"hex{mi}")])
                                ps.read(bank, hv)
                                hs2.append(hv)
                                continue
                            hv = b.op("scalar", lambda s, _m=mi, _b=bank, _h=half:
                                      s.activation(hbit[:, _m, 512 * _h:512 * (_h + 1)],
                                                   pb[_b][:, :], AF.Sigmoid,
                                                   bias=_b1[:, _ei, _m:_m + 1]),
                                      deps=ps.rdeps(bank) + hld + [free.get(f"hex{mi}")])
                            ps.read(bank, hv)
                            hv2 = b.op("vector", lambda v, _m=mi, _b=bank, _c=_b1,
                                       _e=_ei, _h=half: v.scalar_tensor_tensor(
                                           tscr[:, 512 * _h:512 * (_h + 1)], pb[_b][:, :],
                                           _c[:, _e, _m:_m + 1],
                                           hbit[:, _m, 512 * _h:512 * (_h + 1)],
                                           OP.add, OP.mult),
                                       deps=ps.rdeps(bank) + [hv, free.get("tscrfree")])
                            ps.read(bank, hv2)
                            hv3 = b.op("vector", lambda v, _m=mi, _h=half: v.tensor_mul(
                                hbit[:, _m, 512 * _h:512 * (_h + 1)],
                                tscr[:, 512 * _h:512 * (_h + 1)],
                                hbit[:, _m, 512 * _h:512 * (_h + 1)]), deps=[hv2])
                            free["tscrfree"] = hv3
                            hs2.append(hv3)
                        free[f"hex{mi}"] = hs2
                        return hs2
                    def wsrc_h(kg, mp, _w=wh, _ei=ei, _hs=hsz):
                        return bass.AP(_w, _ei * DM * _hs + 256 * mp,
                                       [[_hs, 128], [128 * _hs, 8], [1, 256]])
                    hh = stream_pass(wsrc_h, nmh, 1, xf_rhs, ev_h, xf_done + hwb)
                    if et == "e1":
                        def ev_h1b(mi, pair, hl, _ei=ei):
                            hs2 = []
                            for half in (0, 1):
                                bank = pair[half]
                                hv = b.op("scalar", lambda s, _m=mi, _b=bank, _h=half:
                                          s.activation(
                                              hbit[:, 16 + _m, 512 * _h:512 * (_h + 1)],
                                              pb[_b][:, :], AF.Silu,
                                              bias=eb["e1b2"][:, _ei, _m:_m + 1]),
                                          deps=ps.rdeps(bank) + hld +
                                               [free.get(f"hex{16 + mi}")])
                                ps.read(bank, hv)
                                hs2.append(hv)
                            free[f"hex{16 + mi}"] = hs2
                            return hs2
                        def wsrc_h1b(kg, mp, _ei=ei):
                            return bass.AP(d_e1w2,
                                           _ei * DIN * DIN + kg * 8 * 128 * DIN + 256 * mp,
                                           [[DIN, 128], [128 * DIN, 8], [1, 256]])
                        hh2 = stream_pass(wsrc_h1b, 16, 2, hn_rhs, ev_h1b, hh)
                        out_rhs = lambda kk, half: hbit[:, 16 + kk, 512 * half:512 * (half + 1)]
                        out_dep = hh2
                    elif et == "e0":
                        out_rhs = hn_rhs
                        out_dep = hh
                    else:
                        out_rhs = hn_rhs
                        out_dep = hh
                    first_e = (p == 0)
                    def ev_out(mi, pair, hl, _b2=b2t, _ei=ei, _fe=first_e):
                        hs2 = []
                        for half in (0, 1):
                            bank = pair[half]
                            if _fe:
                                hc = b.op("vector", lambda v, _m=mi, _b=bank, _h=half:
                                          v.scalar_tensor_tensor(
                                              acc[:, _m, 512 * _h:512 * (_h + 1)],
                                              pb[_b][:, :], _b2[:, _ei, _m:_m + 1],
                                              wbc[:, 512 * _h:512 * (_h + 1)],
                                              OP.add, OP.mult),
                                          deps=ps.rdeps(bank) + hld + hwb +
                                               [free.get(f"acc{mi}"), free.get("gsqfree")])
                                ps.read(bank, hc)
                            else:
                                hc1 = b.op("vector", lambda v, _m=mi, _b=bank, _h=half:
                                           v.scalar_tensor_tensor(
                                               tscr[:, 512 * _h:512 * (_h + 1)],
                                               pb[_b][:, :], _b2[:, _ei, _m:_m + 1],
                                               wbc[:, 512 * _h:512 * (_h + 1)],
                                               OP.add, OP.mult),
                                           deps=ps.rdeps(bank) + hld + hwb +
                                                [free.get("tscrfree")])
                                ps.read(bank, hc1)
                                hc = b.op("vector", lambda v, _m=mi, _h=half:
                                          v.tensor_tensor(
                                              acc[:, _m, 512 * _h:512 * (_h + 1)],
                                              acc[:, _m, 512 * _h:512 * (_h + 1)],
                                              tscr[:, 512 * _h:512 * (_h + 1)], OP.add),
                                          deps=[hc1, free.get(f"acc{mi}")])
                                free["tscrfree"] = hc
                            free[f"acc{mi}"] = hc
                            hs2.append(hc)
                        return hs2
                    def wsrc_o(kg, mp, _w=wo, _ei=ei, _hs=hsz):
                        return bass.AP(_w, _ei * _hs * DM + kg * 8 * 128 * DM + 256 * mp,
                                       [[DM, 128], [128 * DM, 8], [1, 256]])
                    ho = stream_pass(wsrc_o, 8, nkg_o, out_rhs, ev_out, out_dep + hwb)
                    free["wbcfree"] = ho
                free["wrows_free"] = free.get("wbcfree")
                x1v = x1buf[:, :, :].rearrange("p a (c b) -> p (a c) b", b=TB)
                hx1l = b.dma("sync", x1v,
                             bass.AP(d_x1, t0, [[NKC * T, 128], [2048, 8], [1, TB]]),
                             deps=[free.get("wbcfree"), free.get("x1buf_free")])
                hf1 = b.op("vector", lambda v: v.tensor_tensor(
                    x1v, x1v, acc[:, :, :], OP.add),
                    deps=[hx1l] + [free.get(f"acc{m}") for m in range(8)])
                hf2 = b.op("vector", lambda v, _t0=t0: v.scalar_tensor_tensor(
                    x1v, xf[:, :, _t0:_t0 + TB], 0.1, x1v, OP.mult, OP.add),
                    deps=[hf1] + xf_done)
                for m in range(8):
                    free[f"acc{m}"] = hf2
                free["gsqfree"] = hf2
                hdo = b.dma("sync",
                            bass.AP(d_out, t0, [[T, 128], [128 * T, NKC], [1, TB]]),
                            x1v, deps=[hf2])
                out_dmas.append(hdo)
                free["x1buf_free"] = hdo
                free["hex16plus"] = hdo
                for m in range(8, 32):
                    free[f"hex{m}"] = [free.get(f"hex{m}"), hdo]
            free["out_dmas"] = out_dmas

        if PHASE_B:
            b.op("sync", lambda s: None, deps=free["out_dmas"], inc=False)
        else:
            x1_done = [free[f"x1w{i}_{m}"] for i in range(NTC) for m in range(8)]
            h_out = b.dma("sync", bass.AP(d_out, 0, [[T, 128], [128 * T, NKC], [1, T]]),
                          x_fm[:, :, :], deps=x1_done)
            b.op("sync", lambda s: None, deps=[h_out], inc=False)

        b.emit(nc, block, sems, strict=STRICT_SYNC)
    return nc


PHASE_B = True
STRICT_SYNC = False
_PCT = (0.3, 0.7)


def _host_consts():
    tri = np.triu(np.ones((128, 128), dtype=np.float32))
    ident = np.eye(128, dtype=np.float32)
    sel = np.zeros((NH, 16 * 128), dtype=np.float32)
    for j in range(16):
        for p in range(128):
            sel[2 * j + p // 64, j * 128 + p] = 1.0
    sel = sel.astype(ml_dtypes.bfloat16)
    ktarg = np.zeros((128, 2), dtype=np.float32)
    ktarg[:, 0] = 14745.0
    ktarg[:, 1] = 14746.0
    return tri, ident, sel, ktarg


def _noise_const():
    import jax
    import jax.numpy as jnp
    cpu = jax.devices("cpu")[0]
    with jax.default_device(cpu):
        n = jax.random.normal(jax.random.key(1), (NCORES, T, 8), dtype=jnp.float32)
        return np.asarray(n)


def kernel(**inp):
    x = np.asarray(inp["L0_attn_x"], dtype=np.float32)
    tri, ident, sel, ktarg = _host_consts()
    n_imp = NCORES * T
    qq = np.float32(np.float32(90.0) / np.float32(100.0)) * np.float32(n_imp - 1)
    low = float(np.floor(qq))
    hw_ = np.float32(qq - np.float32(low))
    lw_ = np.float32(np.float32(1.0) - hw_)
    global _PCT
    _PCT = (float(lw_), float(hw_))
    ktarg = np.zeros((128, 2), dtype=np.float32)
    ktarg[:, 0] = low + 1
    ktarg[:, 1] = low + 2
    sel8 = np.zeros((8, 8, 128), dtype=np.float32)
    for s in range(8):
        sel8[s, s, :] = 1.0
    sel8 = sel8.reshape(8, 8 * 128).astype(ml_dtypes.bfloat16)
    noise = _noise_const()

    conv_w = np.asarray(inp["L0_attn_conv_w"])
    convw_t = np.ascontiguousarray(conv_w[::-1].T)
    convb_t = np.ascontiguousarray(
        np.asarray(inp["L0_attn_conv_b"]).reshape(18, 128).T)
    normw_t = np.ascontiguousarray(
        np.asarray(inp["L0_attn_norm_w"]).reshape(16, 128).T)
    d_rep = np.ascontiguousarray(
        np.repeat(np.asarray(inp["L0_attn_D"]), HD).reshape(16, 128).T)
    lnsc_rep = np.ascontiguousarray(
        np.asarray(inp["L0_ffn_ln_scale"]).reshape(8, 128).T)
    lnof_rep = np.ascontiguousarray(
        np.asarray(inp["L0_ffn_ln_offset"]).reshape(8, 128).T)
    gb1_rep = np.ascontiguousarray(
        np.asarray(inp["L0_ffn_gate_b1"]).reshape(8, 128).T)
    gb2_rep = np.ascontiguousarray(
        np.asarray(inp["L0_ffn_gate_b2"]).reshape(4, 128).T)

    shared = {
        "in_proj_w": np.ascontiguousarray(inp["L0_attn_in_proj_w"]),
        "conv_w_t": convw_t,
        "conv_b_t": convb_t,
        "norm_w_t": normw_t,
        "out_proj_w": np.ascontiguousarray(inp["L0_attn_out_proj_w"]),
        "dt_bias": np.asarray(inp["L0_attn_dt_bias"]).reshape(NH, 1),
        "A_log": np.asarray(inp["L0_attn_A_log"]).reshape(NH, 1),
        "D_rep": d_rep,
        "expert_usage": np.asarray(inp["L0_ffn_expert_usage"]).reshape(8, 1),
        "gate_w1": np.ascontiguousarray(inp["L0_ffn_gate_w1"]),
        "gate_b1_rep": gb1_rep,
        "ln_scale_rep": lnsc_rep,
        "ln_offset_rep": lnof_rep,
        "gate_w2": np.ascontiguousarray(inp["L0_ffn_gate_w2"]),
        "gate_b2_rep": gb2_rep,
        "gate_w3": np.ascontiguousarray(inp["L0_ffn_gate_w3"]),
        "gate_b3_row": np.asarray(inp["L0_ffn_gate_b3"]).reshape(1, 8),
        "aff_w": np.ascontiguousarray(inp["L0_ffn_aff_w"]),
        "aff_b": np.asarray(inp["L0_ffn_aff_b"]).reshape(8, 1),
        "e0_w1": np.ascontiguousarray(inp["L0_ffn_e0_w1"]),
        "e0_w2": np.ascontiguousarray(inp["L0_ffn_e0_w2"]),
        "e1_w1": np.ascontiguousarray(inp["L0_ffn_e1_w1"]),
        "e1_w2": np.ascontiguousarray(inp["L0_ffn_e1_w2"]),
        "e1_w3": np.ascontiguousarray(inp["L0_ffn_e1_w3"]),
        "e2_w1": np.ascontiguousarray(inp["L0_ffn_e2_w1"]),
        "e2_w2": np.ascontiguousarray(inp["L0_ffn_e2_w2"]),
        "ident128": ident,
        "trimask": tri,
        "sel_heads": sel,
        "ktarg": ktarg,
        "sel8": sel8,
        "e0_b1_rep": np.ascontiguousarray(np.asarray(inp["L0_ffn_e0_b1"]).reshape(3, 32, 128).transpose(2, 0, 1)),
        "e0_b2_rep": np.ascontiguousarray(np.asarray(inp["L0_ffn_e0_b2"]).reshape(3, 8, 128).transpose(2, 0, 1)),
        "e1_b1_rep": np.ascontiguousarray(np.asarray(inp["L0_ffn_e1_b1"]).reshape(3, 16, 128).transpose(2, 0, 1)),
        "e1_b2_rep": np.ascontiguousarray(np.asarray(inp["L0_ffn_e1_b2"]).reshape(3, 16, 128).transpose(2, 0, 1)),
        "e1_b3_rep": np.ascontiguousarray(np.asarray(inp["L0_ffn_e1_b3"]).reshape(3, 8, 128).transpose(2, 0, 1)),
        "e2_b1_rep": np.ascontiguousarray(np.asarray(inp["L0_ffn_e2_b1"]).reshape(2, 24, 128).transpose(2, 0, 1)),
        "e2_b2_rep": np.ascontiguousarray(np.asarray(inp["L0_ffn_e2_b2"]).reshape(2, 8, 128).transpose(2, 0, 1)),
    }
    in_maps = []
    for c in range(NCORES):
        m = dict(shared)
        m["x_t"] = np.ascontiguousarray(x[c].T)
        m["noise"] = np.ascontiguousarray(noise[c])
        in_maps.append(m)

    nc = build_program()
    res = run_bass_kernel_spmd(nc, in_maps, core_ids=list(range(NCORES)))
    outs = []
    for c in range(NCORES):
        o = res.results[c]["out_t"].reshape(DM, T)
        outs.append(np.ascontiguousarray(o.T))
    return np.stack(outs).astype(np.float32)


# version 09
# speedup vs baseline: 1.2591x; 1.2591x over previous
import numpy as np
import ml_dtypes
import concourse.bass as bass
import concourse.mybir as mybir
from concourse.bass_utils import run_bass_kernel_spmd
from contextlib import ExitStack

F32 = mybir.dt.float32
F32R = mybir.dt.float32r
BF16 = mybir.dt.bfloat16
AF = mybir.ActivationFunctionType
OP = mybir.AluOpType
AX = mybir.AxisListType

NCORES = 8
DM = 1024
T = 2048
TC = 512
NTC = T // TC
Q = 128
DIN = 2048
DST = 128
NH = 32
HD = 64
CONV_DIM = DIN + 2 * DST
DIP = DIN + CONV_DIM + NH
NKC = DM // 128
EPS = 1e-5

ENGS = ("tensor", "vector", "scalar", "sync", "gpsimd")


class Bld:

    NPOOL = 12

    def __init__(self):
        self.stream = {e: [] for e in ENGS}
        self.cnt = {e: 0 for e in ENGS}
        self.dma_seq = {"sync": 0, "gpsimd": 0}
        self.dma_slot_cnt = {}
        self.engine_vc = {e: {} for e in ENGS}
        self.h_vc = {}

    @staticmethod
    def _flat(deps):
        out = []
        for d in deps:
            if d is None:
                continue
            if isinstance(d, list):
                out.extend(Bld._flat(d))
            else:
                out.append(d)
        return out

    def _merge(self, eng_or_q, deps):
        vc = dict(self.engine_vc[eng_or_q])
        for d in deps:
            for s, v in self.h_vc[d].items():
                if vc.get(s, 0) < v:
                    vc[s] = v
        return vc

    def op(self, eng, fn, deps=(), inc=True):
        deps = self._flat(deps)
        vc = self._merge(eng, deps)
        self.engine_vc[eng] = vc
        waits = dict(vc)
        if inc:
            self.cnt[eng] += 1
            h = (eng, self.cnt[eng])
            vc = dict(vc)
            vc[eng] = self.cnt[eng]
            self.h_vc[h] = vc
            self.engine_vc[eng] = vc
        else:
            h = None
        self.stream[eng].append((waits, fn, "eng" if inc else None))
        return h

    def dma(self, q, out, in_, deps=()):
        deps = self._flat(deps)
        slot = self.dma_seq[q] % self.NPOOL
        self.dma_seq[q] += 1
        key = f"dma_{q}_{slot}"
        prev = self.dma_slot_cnt.get(key, 0)
        vc = self._merge(q, deps)
        if prev:
            if vc.get(key, 0) < 16 * prev:
                vc[key] = 16 * prev
        self.engine_vc[q] = vc
        waits = dict(vc)
        self.dma_slot_cnt[key] = prev + 1
        h = (key, 16 * self.dma_slot_cnt[key])
        hvc = dict(vc)
        hvc[key] = 16 * self.dma_slot_cnt[key]
        self.h_vc[h] = hvc

        def fn(eng, _o=out, _i=in_):
            return eng.dma_start(out=_o, in_=_i)

        self.stream[q].append((waits, fn, ("dma", key)))
        return h

    def emit(self, nc, block, sems, strict=True):
        for eng_name in ENGS:
            stream = self.stream[eng_name]

            def body(eng, _stream=stream, _eng_name=eng_name):
                cur = {}
                nown = 0
                for waits, fn, kind in _stream:
                    for sem_name in sorted(waits):
                        val = waits[sem_name]
                        if (not strict and sem_name == _eng_name
                                and _eng_name in ("tensor",)):
                            continue
                        if cur.get(sem_name, 0) >= val:
                            continue
                        eng.wait_ge(sems[sem_name], val)
                        cur[sem_name] = val
                    inst = fn(eng)
                    if kind == "eng":
                        assert inst is not None
                        inst.then_inc(sems[_eng_name], 1)
                        nown += 1
                    elif isinstance(kind, tuple) and kind[0] == "dma":
                        inst.then_inc(sems[kind[1]], 16)

            getattr(block, eng_name)(body)


class Psum:

    def __init__(self):
        self.writer = [None] * 8
        self.readers = [[] for _ in range(8)]

    def wdeps(self, bank):
        d = list(self.readers[bank])
        if self.writer[bank] is not None:
            d.append(self.writer[bank])
        return d

    def wrote(self, bank, h):
        self.writer[bank] = h
        self.readers[bank] = []

    def rdeps(self, bank):
        return [self.writer[bank]] if self.writer[bank] is not None else []

    def read(self, bank, h):
        self.readers[bank].append(h)


class Arena:

    def __init__(self, tensor, nelem, name):
        self.t = tensor
        self.n = nelem
        self.name = name
        self.off = 0
        self.marks = []

    def alloc(self, shape):
        free = 1
        for s in shape[1:]:
            free *= s
        off = self.off
        assert off + free <= self.n, f"arena {self.name} overflow: {off}+{free}>{self.n}"
        self.off = off + free
        return self.view(off, shape)

    def view(self, off, shape):
        ap = [[self.n, shape[0]]]
        strides = []
        acc = 1
        for s in reversed(shape[1:]):
            strides.append(acc)
            acc *= s
        strides.reverse()
        for s, st in zip(shape[1:], strides):
            ap.append([st, s])
        return bass.AP(self.t, off, ap)

    def push(self):
        self.marks.append(self.off)

    def pop(self):
        self.off = self.marks.pop()


def fview(ap, dtype):
    return ap.bitcast(dtype)



DEBUG_TAPS = ()


def build_program():
    nc = bass.Bass("TRN2", target_bir_lowering=False)

    d_x = nc.dram_tensor("x_t", [DM, T], F32, kind="ExternalInput")
    d_win = nc.dram_tensor("in_proj_w", [DM, DIP], F32, kind="ExternalInput")
    d_convw = nc.dram_tensor("conv_w_t", [CONV_DIM, 4], F32, kind="ExternalInput")
    d_convb = nc.dram_tensor("conv_b_t", [128, 18], F32, kind="ExternalInput")
    d_normw = nc.dram_tensor("norm_w_t", [128, 16], F32, kind="ExternalInput")
    d_wout = nc.dram_tensor("out_proj_w", [DIN, DM], F32, kind="ExternalInput")
    d_dtb = nc.dram_tensor("dt_bias", [NH, 1], F32, kind="ExternalInput")
    d_alog = nc.dram_tensor("A_log", [NH, 1], F32, kind="ExternalInput")
    d_D = nc.dram_tensor("D_rep", [128, 16], F32, kind="ExternalInput")
    d_eu = nc.dram_tensor("expert_usage", [8, 1], F32, kind="ExternalInput")
    d_gw1 = nc.dram_tensor("gate_w1", [DM, DM], F32, kind="ExternalInput")
    d_gb1 = nc.dram_tensor("gate_b1_rep", [128, 8], F32, kind="ExternalInput")
    d_lnsc = nc.dram_tensor("ln_scale_rep", [128, 8], F32, kind="ExternalInput")
    d_lnof = nc.dram_tensor("ln_offset_rep", [128, 8], F32, kind="ExternalInput")
    d_gw2 = nc.dram_tensor("gate_w2", [DM, 512], F32, kind="ExternalInput")
    d_gb2 = nc.dram_tensor("gate_b2_rep", [128, 4], F32, kind="ExternalInput")
    d_gw3 = nc.dram_tensor("gate_w3", [512, 8], F32, kind="ExternalInput")
    d_gb3 = nc.dram_tensor("gate_b3_row", [1, 8], F32, kind="ExternalInput")
    d_affw = nc.dram_tensor("aff_w", [DM, 8], F32, kind="ExternalInput")
    d_affb = nc.dram_tensor("aff_b", [8, 1], F32, kind="ExternalInput")
    d_e0w1 = nc.dram_tensor("e0_w1", [3, DM, 4096], F32, kind="ExternalInput")
    d_e0w2 = nc.dram_tensor("e0_w2", [3, 4096, DM], F32, kind="ExternalInput")
    d_e1w1 = nc.dram_tensor("e1_w1", [3, DM, DIN], F32, kind="ExternalInput")
    d_e1w2 = nc.dram_tensor("e1_w2", [3, DIN, DIN], F32, kind="ExternalInput")
    d_e1w3 = nc.dram_tensor("e1_w3", [3, DIN, DM], F32, kind="ExternalInput")
    d_e2w1 = nc.dram_tensor("e2_w1", [2, DM, 3072], F32, kind="ExternalInput")
    d_e2w2 = nc.dram_tensor("e2_w2", [2, 3072, DM], F32, kind="ExternalInput")
    d_noise = nc.dram_tensor("noise", [T, 8], F32, kind="ExternalInput")
    d_ident = nc.dram_tensor("ident128", [128, 128], F32, kind="ExternalInput")
    d_trimask = nc.dram_tensor("trimask", [128, 128], F32, kind="ExternalInput")
    d_sel = nc.dram_tensor("sel_heads", [NH, 16 * 128], BF16, kind="ExternalInput")
    d_ktarg = nc.dram_tensor("ktarg", [128, 2], F32, kind="ExternalInput")
    d_out = nc.dram_tensor("out_t", [8, 128, T], F32, kind="ExternalOutput")

    d_imp_in = nc.dram_tensor("imp_in", [1, T], F32)
    d_imp_out = nc.dram_tensor("imp_out", [NCORES, T], F32)
    d_l1 = nc.dram_tensor("l1_scratch", [1, T], F32)
    d_x1 = nc.dram_tensor("x1_scratch", [128, NKC * T], F32)
    d_sel8 = nc.dram_tensor("sel8", [8, 8 * 128], BF16, kind="ExternalInput")
    d_e0b1 = nc.dram_tensor("e0_b1_rep", [128, 3, 32], F32, kind="ExternalInput")
    d_e0b2 = nc.dram_tensor("e0_b2_rep", [128, 3, 8], F32, kind="ExternalInput")
    d_e1b1 = nc.dram_tensor("e1_b1_rep", [128, 3, 16], F32, kind="ExternalInput")
    d_e1b2 = nc.dram_tensor("e1_b2_rep", [128, 3, 16], F32, kind="ExternalInput")
    d_e1b3 = nc.dram_tensor("e1_b3_rep", [128, 3, 8], F32, kind="ExternalInput")
    d_e2b1 = nc.dram_tensor("e2_b1_rep", [128, 2, 24], F32, kind="ExternalInput")
    d_e2b2 = nc.dram_tensor("e2_b2_rep", [128, 2, 8], F32, kind="ExternalInput")

    b = Bld()
    ps = Psum()
    tap_d = {}
    for tn, tshape in DEBUG_TAPS:
        tap_d[tn] = nc.dram_tensor("tap_" + tn, tshape, F32, kind="ExternalOutput")

    with ExitStack() as stk:
        e = stk.enter_context
        x_fm = e(nc.sbuf_tensor("x_fm", [128, NKC, T], F32))
        af32_t = e(nc.sbuf_tensor("af32", [128, 4096], F32))
        ab16_t = e(nc.sbuf_tensor("ab16", [128, 44672], BF16))
        af = Arena(af32_t, 4096, "f32")
        ab = Arena(ab16_t, 44672, "b16")

        ones_col = e(nc.sbuf_tensor("ones_col", [128, 1], F32))
        ones_row = e(nc.sbuf_tensor("ones_row", [1, 128], F32))
        ones_sq = e(nc.sbuf_tensor("ones_sq", [128, 128], F32))
        ident = e(nc.sbuf_tensor("ident", [128, 128], F32))
        trimask = e(nc.sbuf_tensor("trimask_s", [128, 128], F32))
        sel = e(nc.sbuf_tensor("sel_s", [NH, 16, 128], BF16))
        convw = e(nc.sbuf_tensor("convw_s", [128, 18, 4], F32))
        convb = e(nc.sbuf_tensor("convb_s", [128, 18], F32))
        normw = e(nc.sbuf_tensor("normw_s", [128, 16], F32))
        drep = e(nc.sbuf_tensor("drep_s", [128, 16], F32))
        dtb = e(nc.sbuf_tensor("dtb_s", [NH, 1], F32))
        acol = e(nc.sbuf_tensor("acol_s", [NH, 1], F32))
        lnsc = e(nc.sbuf_tensor("lnsc_s", [128, 8], F32))
        lnof = e(nc.sbuf_tensor("lnof_s", [128, 8], F32))
        gb1 = e(nc.sbuf_tensor("gb1_s", [128, 8], F32))
        gb2 = e(nc.sbuf_tensor("gb2_s", [128, 4], F32))
        gb3bc = e(nc.sbuf_tensor("gb3bc_s", [128, 8], F32))
        gb3row = e(nc.sbuf_tensor("gb3row_s", [1, 8], F32))
        affb = e(nc.sbuf_tensor("affb_s", [8, 1], F32))
        eu_sb = e(nc.sbuf_tensor("eu_s", [8, 1], F32))
        noise_sb = e(nc.sbuf_tensor("noise_s", [128, 16, 8], F32))
        ktarg = e(nc.sbuf_tensor("ktarg_s", [128, 2], F32))
        impall = e(nc.sbuf_tensor("impall_s", [128, 128], F32))
        bis = e(nc.sbuf_tensor("bis_s", [128, 16], F32))
        wrows = e(nc.sbuf_tensor("wrows_s", [8, T], BF16))
        affbc = e(nc.sbuf_tensor("affbc_s", [128, 8], F32))
        nsc = e(nc.sbuf_tensor("nsc_s", [128, 1], F32))
        thr_sb = e(nc.sbuf_tensor("thr_s", [1, 8], F32))
        scl_sb = e(nc.sbuf_tensor("scl_s", [1, 8], F32))

        ln1_t = e(nc.sbuf_tensor("ln1_t", [128, NKC, TC], F32R))
        winbuf_t = e(nc.sbuf_tensor("winbuf_t", [128, 3, NKC, 128], F32R))
        pb = [e(nc.psum_tensor(f"pb{i}", [128, 512], F32)) for i in range(8)]

        sems = {}
        for s in ENGS:
            sems[s] = e(nc.semaphore("sem_" + s))
        for q in ("sync", "gpsimd"):
            for i in range(Bld.NPOOL):
                sems[f"dma_{q}_{i}"] = e(nc.semaphore(f"sem_dma_{q}_{i}"))
        cc_sem = e(nc.semaphore("sem_cc"))

        block = e(nc.Block())

        ln1 = ln1_t[:, :, :]
        winbuf = [winbuf_t[:, i, :, :] for i in range(3)]
        dtrow = af.alloc([NH, TC])
        crow = af.alloc([NH, TC])
        tcm = af.alloc([NH, 128])
        urow = af.alloc([NH, 128])
        emrow = af.alloc([1, 32])
        bbrow = af.alloc([1, 32])
        wmat = af.alloc([128, 32])
        emat = af.alloc([128, 32])
        bbmat = af.alloc([128, 32])
        row_a = af.alloc([1, TC])
        row_b = af.alloc([1, TC])
        row_c = af.alloc([1, TC])
        af.push()

        o_sz = ab.off
        sz = ab.alloc([128, 16, TC])
        xbc = ab.alloc([128, 18, TC])
        raw = [ab.alloc([128, 520]) for _ in range(2)]
        o_g = ab.off
        g_sb = ab.alloc([128, 16, TC])
        smat = ab.alloc([128, DIN])
        xpp = ab.alloc([128, DIN])
        xppb = ab.alloc([128, DIN])
        o_wout = ab.off
        woutbuf = [ab.alloc([128, 16, 128]) for _ in range(2)]
        yscr = ab.alloc([128, DIN])
        o_vfull = ab.off
        vfull = ab.alloc([128, 16, 128])
        gmref = ab.alloc([128, 128])
        btm = ab.alloc([128, 128])
        o_vbf = ab.off
        vbf = ab.alloc([NH, 128])
        halo = ab.alloc([128, 18, 4])
        b16_amark = ab.off

        FR = lambda ap: fview(ap, F32R)

        ld0 = []
        ld0.append(b.dma("sync", ident[:, :], d_ident[:, :]))
        ld0.append(b.dma("sync", trimask[:, :], d_trimask[:, :]))
        ld0.append(b.dma("sync", sel[:, :, :],
                         bass.AP(d_sel, 0, [[16 * 128, NH], [128, 16], [1, 128]])))
        ld0.append(b.dma("sync", convw[:, :, :],
                         bass.AP(d_convw, 0, [[4, 128], [128 * 4, 18], [1, 4]])))
        ld0.append(b.dma("sync", convb[:, :], d_convb[:, :]))
        ld0.append(b.dma("sync", normw[:, :], d_normw[:, :]))
        ld0.append(b.dma("sync", drep[:, :], d_D[:, :]))
        ld0.append(b.dma("sync", dtb[:, :], d_dtb[:, :]))
        ld0.append(b.dma("sync", acol[:, :], d_alog[:, :]))
        ld0.append(b.dma("sync", lnsc[:, :], d_lnsc[:, :]))
        ld0.append(b.dma("sync", lnof[:, :], d_lnof[:, :]))
        ld0.append(b.dma("sync", gb1[:, :], d_gb1[:, :]))
        ld0.append(b.dma("sync", gb2[:, :], d_gb2[:, :]))
        ld0.append(b.dma("sync", gb3row[:, :], d_gb3[:, :]))
        ld0.append(b.dma("sync", affb[:, :], d_affb[:, :]))
        ld0.append(b.dma("sync", eu_sb[:, :], d_eu[:, :]))
        ld0.append(b.dma("sync", ktarg[:, :], d_ktarg[:, :]))
        ld0.append(b.dma("sync", noise_sb[:, :, :],
                         bass.AP(d_noise, 0, [[8, 128], [128 * 8, 16], [1, 8]])))
        h_x = b.dma("sync", x_fm[:, :, :],
                    bass.AP(d_x, 0, [[T, 128], [128 * T, NKC], [1, T]]))

        h_ones = b.op("vector", lambda v: v.memset(ones_col[:, :], 1.0))
        h_onesrow = b.op("vector", lambda v: v.memset(ones_row[:, :], 1.0))
        h_onesq = b.op("vector", lambda v: v.memset(ones_sq[:, :], 1.0))
        h_halo0 = b.op("vector", lambda v: v.memset(halo[:, :, :], 0.0))
        h_smat0 = b.op("vector", lambda v: v.memset(smat[:, :], 0.0))
        h_acol = b.op("scalar", lambda s: s.activation(acol[:, :], acol[:, :], AF.Exp),
                      deps=[ld0[8]])
        h_acol = b.op("vector", lambda v: v.tensor_scalar_mul(acol[:, :], acol[:, :], -1.0),
                      deps=[h_acol])
        h = b.op("tensor", lambda t: t.matmul(pb[6][:, :8], ones_row[:, :], gb3row[:, :],
                                              start=True, stop=True),
                 deps=[ld0[13], h_onesrow] + ps.wdeps(6))
        ps.wrote(6, h)
        h_gb3 = b.op("vector", lambda v: v.tensor_copy(gb3bc[:, :], pb[6][:, :8]),
                     deps=ps.rdeps(6))
        ps.read(6, h_gb3)

        dttmp = af.alloc([NH, TC])
        zrow = af.alloc([NH, 128])
        c2 = af.alloc([NH, 2])
        identb = ab.alloc([128, 128])
        onescb = ab.alloc([128, 1])
        o_sqscr = ab.off
        sqscr = ab.alloc([128, 2, TC])
        o_cacc = ab.off
        convacc = ab.alloc([128, 2, TC])

        h_zrow = b.op("vector", lambda v: v.memset(zrow, 0.0))
        h_identb = b.op("vector", lambda v: v.tensor_copy(identb, ident[:, :]), deps=[ld0[0]])
        h_onescb = b.op("vector", lambda v: v.memset(onescb, 1.0))

        free = {}

        def win_src(mi):
            c0 = 128 * mi
            w = 128 if mi < 34 else 32
            return bass.AP(d_win, c0, [[DIP, 128], [128 * DIP, NKC], [1, w]]), w

        def wout_src(mi):
            c0 = 128 * mi
            return bass.AP(d_wout, c0, [[DM, 128], [128 * DM, 16], [1, 128]])

        LNW = 1.0 / DM
        hs_dS = {"valid": False}

        def emit_lnstats(src_slices, nk, scale, bank_s, bank_q, deps_in, ones_bf=False):
            w = src_slices[0].shape[-1]
            o_lhs = onescb if ones_bf else ones_col
            o_dep = h_onescb if ones_bf else h_ones
            hsum = None
            for k in range(nk):
                hsum = b.op("tensor",
                            lambda t, _k=k, _w=w: t.matmul(
                                pb[bank_s][:1, :_w], o_lhs[:, :], src_slices[_k],
                                start=(_k == 0), stop=(_k == nk - 1)),
                            deps=(ps.wdeps(bank_s) + deps_in + [o_dep]) if k == 0 else [],
                            inc=(k == nk - 1))
            ps.wrote(bank_s, hsum)
            hsq = None
            for k in range(nk):
                hq = b.op("scalar",
                          lambda s, _k=k, _w=w: s.activation(
                              sqscr[:, _k % 2, :_w], src_slices[_k], AF.Square),
                          deps=[free.get(f"sq{k % 2}")] + (deps_in if k < 2 else []))
                hsq = b.op("tensor",
                           lambda t, _k=k, _w=w: t.matmul(
                               pb[bank_q][:1, :_w], onescb, sqscr[:, _k % 2, :_w],
                               start=(_k == 0), stop=(_k == nk - 1)),
                           deps=([hq] + ps.wdeps(bank_q) + [h_onescb]) if k == 0 else [hq],
                           inc=True)
                free[f"sq{k % 2}"] = hsq
            ps.wrote(bank_q, hsq)
            return hsum, hsq

        def emit_ln_rows(bank_s, bank_q, n, eps):
            w = TC
            h1 = b.op("vector", lambda v: v.tensor_scalar(
                row_a[:, :], pb[bank_s][:1, :], 1.0 / n, None, OP.mult),
                deps=ps.rdeps(bank_s) + [free.get("rows_free")])
            ps.read(bank_s, h1)
            h2 = b.op("vector", lambda v: v.tensor_mul(row_b[:, :], row_a[:, :], row_a[:, :]),
                      deps=[h1])
            h3 = b.op("vector", lambda v: v.scalar_tensor_tensor(
                row_b[:, :], pb[bank_q][:1, :], 1.0 / n, row_b[:, :], OP.mult, OP.subtract),
                deps=ps.rdeps(bank_q) + [h2])
            ps.read(bank_q, h3)
            h4 = b.op("vector", lambda v: v.tensor_scalar_add(row_b[:, :], row_b[:, :], eps),
                      deps=[h3])
            h5 = b.op("vector", lambda v: v.reciprocal(row_b[:, :], row_b[:, :]), deps=[h4])
            h6 = b.op("scalar", lambda s: s.activation(row_b[:, :], row_b[:, :], AF.Sqrt),
                      deps=[h5])
            h7 = b.op("vector", lambda v: v.tensor_scalar(
                row_c[:, :], row_a[:, :], -1.0, None, OP.mult), deps=[h1])
            h8 = b.op("vector", lambda v: v.tensor_mul(row_c[:, :], row_c[:, :], row_b[:, :]),
                      deps=[h6, h7])
            return h6, h8

        def emit_bcast_row(row_ap, bank, deps_in, width=TC):
            h = b.op("tensor", lambda t: t.matmul(
                pb[bank][:, :width], ones_row[:, :], row_ap, start=True, stop=True),
                deps=ps.wdeps(bank) + deps_in + [h_onesrow])
            ps.wrote(bank, h)
            return h

        for tci in range(NTC):
            t0 = tci * TC
            xs = [x_fm[:, k, t0:t0 + TC] for k in range(NKC)]
            hstat = emit_lnstats(xs, NKC, LNW, 0, 1, [h_x])
            hrows = emit_ln_rows(0, 1, DM, EPS)
            if tci == 0 and tap_d:
                if "rowa" in tap_d:
                    free["tapr"] = [b.dma("gpsimd", tap_d["rowa"].ap(), row_a[:, :], deps=list(hrows)),
                                    b.dma("gpsimd", tap_d["rowb"].ap(), row_b[:, :], deps=list(hrows)),
                                    b.dma("gpsimd", tap_d["rowc"].ap(), row_c[:, :], deps=list(hrows))]
            hr = emit_bcast_row(row_b[:, :], 0, [hrows[0], free.get("tapr")])
            hn = emit_bcast_row(row_c[:, :], 1, [hrows[1], free.get("tapr")])
            free["rows_free"] = [hr, hn]
            ln1_done = []
            for k in range(NKC):
                ha = b.op("vector", lambda v, _k=k, _t0=t0: v.tensor_tensor(
                    FR(ln1[:, _k, :]), x_fm[:, _k, _t0:_t0 + TC], pb[0][:, :], OP.mult),
                    deps=ps.rdeps(0) + [free.get("ln1f")])
                ps.read(0, ha)
                hb = b.op("vector", lambda v, _k=k: v.tensor_tensor(
                    FR(ln1[:, _k, :]), FR(ln1[:, _k, :]), pb[1][:, :], OP.add),
                    deps=ps.rdeps(1) + [ha])
                ps.read(1, hb)
                ln1_done.append(hb)

            wload = {}
            for mi in range(3):
                srcap, wdt = win_src(mi)
                wload[mi] = b.dma("gpsimd", FR(winbuf[mi % 3][:, :, :wdt]), srcap,
                                  deps=[free.get(f"winb{mi % 3}")])
            banks_ip = [2, 3, 0, 1]
            sz_done = []
            for mi in range(35):
                bank = banks_ip[mi % 4]
                wdt = 128 if mi < 34 else 32
                dmm = ps.wdeps(bank) + [wload[mi]] + ln1_done
                hmm = None
                for k in range(NKC):
                    hmm = b.op("tensor", lambda t, _mi=mi, _k=k, _b=bank, _w=wdt: t.matmul(
                        pb[_b][:_w, :], FR(winbuf[_mi % 3][:, _k, :_w]), FR(ln1[:, _k, :]),
                        start=(_k == 0), stop=(_k == NKC - 1)),
                        deps=dmm if k == 0 else [], inc=(k == NKC - 1))
                ps.wrote(bank, hmm)
                free[f"winb{mi % 3}"] = hmm
                if mi + 3 < 35:
                    srcap2, wdt2 = win_src(mi + 3)
                    wload[mi + 3] = b.dma("gpsimd", FR(winbuf[(mi + 3) % 3][:, :, :wdt2]),
                                          srcap2, deps=[hmm])
                if mi < 16:
                    hv = b.op("scalar", lambda s, _mi=mi, _b=bank: s.activation(
                        sz[:, _mi, :], pb[_b][:, :], AF.Silu),
                        deps=ps.rdeps(bank) + [free.get(f"sz{mi}")])
                    ps.read(bank, hv)
                    free[f"sz{mi}"] = None
                    sz_done.append(hv)
                elif mi < 34:
                    ci = mi - 16
                    rb = ci % 2
                    he = b.op("vector", lambda v, _b=bank, _rb=rb: v.tensor_copy(
                        raw[_rb][:, 3:3 + TC], pb[_b][:, :]),
                        deps=ps.rdeps(bank) + [free.get(f"raw{rb}")])
                    ps.read(bank, he)
                    hh = b.op("vector", lambda v, _ci=ci, _rb=rb: v.tensor_copy(
                        raw[_rb][:, 0:3], halo[:, _ci, 0:3]),
                        deps=[he, h_halo0, free.get(f"halo{ci}")])
                    ab_ = ci % 2
                    h1 = b.op("vector", lambda v, _ci=ci, _rb=rb, _a=ab_: v.tensor_scalar(
                        convacc[:, _a, :], raw[_rb][:, 0:TC], convw[:, _ci, 0:1], None, OP.mult),
                        deps=[hh, ld0[3], free.get(f"cacc{ab_}")])
                    for kk in range(1, 4):
                        h1 = b.op("vector", lambda v, _ci=ci, _kk=kk, _rb=rb, _a=ab_:
                                  v.scalar_tensor_tensor(
                                      convacc[:, _a, :], raw[_rb][:, _kk:_kk + TC],
                                      convw[:, _ci, _kk:_kk + 1], convacc[:, _a, :],
                                      OP.mult, OP.add), deps=[h1])
                    hu = b.op("vector", lambda v, _ci=ci, _rb=rb: v.tensor_copy(
                        halo[:, _ci, 0:3], raw[_rb][:, TC:TC + 3]), deps=[he, hh])
                    free[f"halo{ci}"] = hu
                    free[f"raw{rb}"] = hu
                    hsl = b.op("scalar", lambda s, _ci=ci, _a=ab_: s.activation(
                        xbc[:, _ci, :], convacc[:, _a, :], AF.Silu,
                        bias=convb[:, _ci:_ci + 1]),
                        deps=[h1, ld0[4], free.get(f"xbcf{ci}")])
                    free[f"cacc{ab_}"] = hsl
                    free[f"xbc{ci}"] = hsl
                else:
                    hdt = b.op("scalar", lambda s, _b=bank: s.activation(
                        dttmp[:, :], pb[_b][:32, :], AF.Exp, bias=dtb[:, :]),
                        deps=ps.rdeps(bank) + [ld0[7], free.get("dtrow")])
                    ps.read(bank, hdt)
                    hdt = b.op("scalar", lambda s: s.activation(
                        dtrow[:, :], dttmp[:, :], AF.Ln, bias=1.0), deps=[hdt])
                    free["dt_ready"] = hdt
                    free["ln1f"] = hmm
            hal = b.op("vector", lambda v: v.tensor_scalar(
                dttmp[:, :], dtrow[:, :], acol[:, :], None, OP.mult),
                deps=[free["dt_ready"], h_acol])
            c_ready = []
            for ql in range(4):
                hc = b.op("vector", lambda v, _q=ql: v.tensor_tensor_scan(
                    crow[:, _q * 128:(_q + 1) * 128], dttmp[:, _q * 128:(_q + 1) * 128],
                    zrow, 0.0, OP.add, OP.add), deps=[hal, h_zrow])
                c_ready.append(hc)

            def fb(base, extra, dims):
                return bass.AP(base.tensor, base.offset + extra, [list(base.ap[0])] + dims)

            for ql in range(4):
                gq = tci * 4 + ql
                qs = ql * 128
                mcol = crow[:, qs + 63:qs + 64]
                h_tcm = b.op("vector", lambda v, _q=qs: v.tensor_scalar(
                    tcm[:, :], crow[:, _q:_q + 128], crow[:, _q + 63:_q + 64], None,
                    OP.subtract), deps=[c_ready[ql], free.get("tcm_free")])
                h_vbf = b.op("scalar", lambda s: s.activation(vbf[:, :], tcm[:, :], AF.Exp),
                             deps=[h_tcm, free.get("vbf_free")])
                h_ur = b.op("scalar", lambda s: s.activation(urow[:, :], tcm[:, :], AF.Exp,
                                                             scale=-1.0),
                            deps=[h_tcm, free.get("urow_free")])
                h_ur = b.op("vector", lambda v, _q=qs: v.tensor_tensor(
                    urow[:, :], urow[:, :], dtrow[:, _q:_q + 128], OP.mult), deps=[h_ur])
                free["tcm_free"] = h_ur
                h_c2a = b.op("vector", lambda v, _q=qs: v.tensor_copy(
                    c2[:, 0:1], crow[:, _q + 63:_q + 64]), deps=[c_ready[ql], free.get("c2f")])
                h_c2b = b.op("vector", lambda v, _q=qs: v.tensor_scalar(
                    c2[:, 1:2], crow[:, _q + 127:_q + 128], crow[:, _q + 63:_q + 64], None,
                    OP.subtract), deps=[c_ready[ql], h_c2a])
                hw0 = b.op("tensor", lambda t: t.transpose(pb[0][:, 0:32], urow[:, :],
                                                           ident[:32, :32]),
                           deps=ps.wdeps(0) + [h_ur, ld0[0]])
                ps.wrote(0, hw0)
                hw1 = b.op("tensor", lambda t: t.transpose(pb[0][:1, 32:64], c2[:, 0:1],
                                                           ident[:32, :32]),
                           deps=[h_c2a])
                ps.wrote(0, hw1)
                hw2 = b.op("tensor", lambda t: t.transpose(pb[0][:1, 64:96], c2[:, 1:2],
                                                           ident[:32, :32]),
                           deps=[h_c2b])
                ps.wrote(0, hw2)
                h_wm = b.op("vector", lambda v: v.tensor_copy(wmat[:, :], pb[0][:, 0:32]),
                            deps=ps.rdeps(0) + [free.get("wmat_free")])
                ps.read(0, h_wm)
                h_em = b.op("scalar", lambda s: s.activation(emrow[:, :], pb[0][:1, 32:64],
                                                             AF.Exp),
                            deps=ps.rdeps(0) + [free.get("emrow_free")])
                ps.read(0, h_em)
                h_bb = b.op("scalar", lambda s: s.activation(bbrow[:, :], pb[0][:1, 64:96],
                                                             AF.Exp),
                            deps=ps.rdeps(0) + [free.get("bbrow_free")])
                ps.read(0, h_bb)
                free["c2f"] = h_bb
                hbe = b.op("tensor", lambda t: t.matmul(pb[0][:, 96:128], ones_row[:, :],
                                                        emrow[:, :], start=True, stop=True),
                           deps=ps.wdeps(0) + [h_em, h_onesrow])
                ps.wrote(0, hbe)
                hbb = b.op("tensor", lambda t: t.matmul(pb[0][:, 128:160], ones_row[:, :],
                                                        bbrow[:, :], start=True, stop=True),
                           deps=[h_bb])
                ps.wrote(0, hbb)
                h_emat = b.op("vector", lambda v: v.tensor_copy(emat[:, :], pb[0][:, 96:128]),
                              deps=ps.rdeps(0) + [free.get("emat_free")])
                ps.read(0, h_emat)
                h_bbm = b.op("vector", lambda v: v.tensor_copy(bbmat[:, :], pb[0][:, 128:160]),
                             deps=ps.rdeps(0) + [free.get("bbmat_free")])
                ps.read(0, h_bbm)
                free["emrow_free"] = hbe
                free["bbrow_free"] = hbb
                free["urow_free"] = hw0
                h_sm = []
                if hs_dS["valid"]:
                    for bb2 in range(4):
                        hsm = b.op("vector", lambda v, _b=bb2: v.tensor_tensor(
                            smat[:, 512 * _b:512 * (_b + 1)],
                            pb[4 + _b][:, :],
                            fb(emat, 8 * _b, [[1, 8], [0, 64]]), OP.mult),
                            deps=ps.rdeps(4 + bb2) + [h_emat, free.get("smat_free")])
                        ps.read(4 + bb2, hsm)
                        h_sm.append(hsm)
                else:
                    h_sm = [h_smat0]
                h_xpp = []
                for ggrp in range(4):
                    bank = 1 + (ggrp % 2)
                    hts = None
                    dport = ps.wdeps(bank)
                    for jj in range(4):
                        j = 4 * ggrp + jj
                        hts = b.op("tensor", lambda t, _j=j, _jj=jj, _b=bank, _q=qs:
                                   t.transpose(pb[_b][:, 64 * _jj:64 * (_jj + 1)].bitcast(BF16),
                                               xbc[:, _j, _q:_q + 128], identb),
                                   deps=(dport + [free[f"xbc{j}"], h_identb]) if jj == 0
                                   else [free[f"xbc{j}"]],
                                   inc=(jj == 3))
                        ps.wrote(bank, hts)
                    hev = b.op("vector", lambda v, _g=ggrp, _b=bank: v.tensor_tensor(
                        xpp[:, 512 * _g:512 * (_g + 1)].rearrange("p (a b c) -> p a b c",
                                                                  a=4, b=2),
                        pb[_b][:, 0:256].bitcast(BF16).rearrange("p (a b c) -> p a b c",
                                                                 a=4, b=2),
                        fb(wmat, 8 * _g, [[2, 4], [1, 2], [0, 64]]), OP.mult),
                        deps=ps.rdeps(bank) + [h_wm, free.get("xpp_free")])
                    ps.read(bank, hev)
                    h_xpp.append(hev)
                h_xppb = b.op("vector", lambda v: v.tensor_tensor(
                    xppb[:, :].rearrange("p (a b) -> p a b", a=32),
                    xpp[:, :].rearrange("p (a b) -> p a b", a=32),
                    fb(bbmat, 0, [[1, 32], [0, 64]]), OP.mult),
                    deps=h_xpp + [h_bbm, free.get("xppb_free")])
                hbt = b.op("tensor", lambda t, _q=qs: t.transpose(
                    pb[3][:, 0:64].bitcast(BF16), xbc[:, 16, _q:_q + 128], identb),
                    deps=ps.wdeps(3) + [free["xbc16"], h_identb])
                ps.wrote(3, hbt)
                hg = b.op("tensor", lambda t, _q=qs: t.matmul(
                    pb[3][:, 128:256], xbc[:, 16, _q:_q + 128], xbc[:, 17, _q:_q + 128],
                    start=True, stop=True), deps=[free["xbc17"]])
                ps.wrote(3, hg)
                h_btm = b.op("vector", lambda v: v.tensor_copy(btm[:, :],
                                                               pb[3][:, 0:64].bitcast(BF16)),
                             deps=ps.rdeps(3) + [free.get("btm_free")])
                ps.read(3, h_btm)
                h_gm = b.op("vector", lambda v: v.tensor_tensor(
                    gmref[:, :], pb[3][:, 128:256], trimask[:, :], OP.mult),
                    deps=ps.rdeps(3) + [ld0[1], free.get("gm_free")])
                ps.read(3, h_gm)
                h_vf = []
                for rr in range(4):
                    bank = 1 + (rr % 2)
                    hv = None
                    dport = ps.wdeps(bank)
                    for jj in range(4):
                        j = 4 * rr + jj
                        hv = b.op("tensor", lambda t, _j=j, _jj=jj, _b=bank: t.matmul(
                            pb[_b][:, 128 * _jj:128 * (_jj + 1)], sel[:, _j, :], vbf[:, :],
                            start=True, stop=True),
                            deps=(dport + [h_vbf, ld0[2]]) if jj == 0 else [],
                            inc=(jj == 3))
                        ps.wrote(bank, hv)
                    hve = b.op("vector", lambda v, _r=rr, _b=bank: v.tensor_copy(
                        vfull[:, 4 * _r:4 * (_r + 1), :],
                        pb[_b][:, :].rearrange("p (a b) -> p a b", a=4)),
                        deps=ps.rdeps(bank) + [free.get("vfull_free")])
                    ps.read(bank, hve)
                    h_vf.append(hve)
                free["vbf_free"] = h_vf[3]
                h_ymm = [None] * 4
                for hp in range(16):
                    bank = hp // 4
                    reg = (hp % 4) * 128
                    first_in_bank = (hp % 4 == 0)
                    dmm = (ps.wdeps(bank) if first_in_bank else []) + \
                          [h_xpp[hp // 4], h_gm]
                    hin = b.op("tensor", lambda t, _hp=hp, _b=bank, _r=reg, _gq=gq: t.matmul(
                        pb[_b][:, _r:_r + 128], xpp[:, 128 * _hp:128 * (_hp + 1)],
                        gmref[:, :], start=True, stop=(_gq == 0)),
                        deps=dmm, inc=(gq == 0))
                    if gq > 0:
                        hin = b.op("tensor", lambda t, _hp=hp, _b=bank, _r=reg, _q=qs:
                                   t.matmul(pb[_b][:, _r:_r + 128],
                                            smat[:, 128 * _hp:128 * (_hp + 1)],
                                            xbc[:, 17, _q:_q + 128],
                                            start=False, stop=True),
                                   deps=h_sm + [free["xbc17"]])
                    ps.wrote(bank, hin)
                    h_ymm[bank] = hin
                for bb2 in range(4):
                    hds = b.op("tensor", lambda t, _b=bb2: t.matmul(
                        pb[4 + _b][:, :], btm[:, :], xppb[:, 512 * _b:512 * (_b + 1)],
                        start=True, stop=True),
                        deps=ps.wdeps(4 + bb2) + [h_btm, h_xppb])
                    ps.wrote(4 + bb2, hds)
                hs_dS["valid"] = True
                h_ya = []
                for bank in range(4):
                    hy = b.op("vector", lambda v, _b=bank: v.tensor_tensor(
                        yscr[:, 512 * _b:512 * (_b + 1)].rearrange("p (a b) -> p a b", a=4),
                        pb[_b][:, :].rearrange("p (a b) -> p a b", a=4),
                        vfull[:, 4 * _b:4 * (_b + 1), :], OP.mult),
                        deps=ps.rdeps(bank) + [h_vf[bank], free.get("yscr_free")])
                    ps.read(bank, hy)
                    h_ya.append(hy)
                ds_handles = [ps.writer[4 + i] for i in range(4)]
                h_dxp = b.op("vector", lambda v, _q=qs: v.tensor_tensor(
                    xppb[:, :].rearrange("p (a b) -> p a b", a=16),
                    xbc[:, 0:16, _q:_q + 128],
                    bass.AP(drep, 0, [[16, 128], [1, 16], [0, 128]]), OP.mult),
                    deps=ds_handles + [ld0[6]])
                h_y = b.op("vector", lambda v: v.tensor_tensor(
                    yscr[:, :], yscr[:, :], xppb[:, :], OP.add),
                    deps=h_ya + [h_dxp])
                free["xppb_free"] = h_y
                h_g = b.op("vector", lambda v, _q=qs: v.tensor_tensor(
                    g_sb[:, :, _q:_q + 128],
                    yscr[:, :].rearrange("p (a b) -> p a b", a=16),
                    sz[:, :, _q:_q + 128], OP.mult),
                    deps=[h_y, free.get("g_free")])
                free["yscr_free"] = h_g
                free["xpp_free"] = h_ymm
                free["wmat_free"] = h_xpp[3]
                free["emat_free"] = h_sm[-1]
                free["bbmat_free"] = h_xppb
                free["btm_free"] = ds_handles
                free["gm_free"] = h_ymm[3]
                free["vfull_free"] = h_ya
                free["smat_free"] = h_ymm[3]
                if ql == 3:
                    free["dtrow"] = h_ur
                    for j in range(18):
                        free[f"xbcf{j}"] = [h_dxp, h_ymm[3]]
                    for mi2 in range(16):
                        free[f"sz{mi2}"] = h_g

            if tci == 0 and tap_d:
                def emit_tap(name, src_ap, deps):
                    if name not in tap_d:
                        return None
                    hh = b.dma("gpsimd", tap_d[name].ap(), src_ap, deps=deps)
                    free["tapblock"] = (free.get("tapblock") or []) + [hh]
                    return hh
                emit_tap("ln1", ln1, [free.get("ln1f"), free["dt_ready"]])
                emit_tap("sz", sz, sz_done)
                emit_tap("dt", dtrow[:, :], [free["dt_ready"], hal])
                emit_tap("c", crow[:, :], c_ready)
                emit_tap("xbc", xbc, [free.get("xbc17")])
                emit_tap("g", g_sb, [free.get("g_free"), free["yscr_free"]])
                if free.get("tapblock"):
                    tb = free["tapblock"]
                    free["ln1f"] = Bld._flat([free.get("ln1f")]) + tb
                    for j in range(18):
                        free[f"xbcf{j}"] = (free.get(f"xbcf{j}") or []) + tb
                    for m2 in range(16):
                        free[f"sz{m2}"] = Bld._flat([free.get(f"sz{m2}")]) + tb
                    free["g_free"] = Bld._flat([free.get("g_free")]) + tb

            yv = yscr[:, :].rearrange("p (a b) -> p a b", a=4)
            hsq = None
            for gq4 in range(4):
                hq = b.op("scalar", lambda s, _g=gq4: s.activation(
                    yv, g_sb[:, 4 * _g:4 * (_g + 1), :], AF.Square),
                    deps=[free.get("gsq4"), free["yscr_free"]])
                for kk in range(4):
                    k = 4 * gq4 + kk
                    dsq = [hq] if kk == 0 else []
                    if k == 0:
                        dsq = dsq + ps.wdeps(0)
                    hsq = b.op("tensor", lambda t, _k=k, _kk=kk: t.matmul(
                        pb[0][:1, :], onescb, yv[:, _kk, :],
                        start=(_k == 0), stop=(_k == 15)),
                        deps=dsq, inc=True)
                free["gsq4"] = hsq
            ps.wrote(0, hsq)
            free["yscr_free"] = hsq
            h1 = b.op("vector", lambda v: v.tensor_scalar(
                row_a[:, :], pb[0][:1, :], 1.0 / DIN, None, OP.mult),
                deps=ps.rdeps(0) + [free.get("rows_free")])
            ps.read(0, h1)
            h2 = b.op("vector", lambda v: v.tensor_scalar_add(row_a[:, :], row_a[:, :], EPS),
                      deps=[h1])
            h3 = b.op("vector", lambda v: v.reciprocal(row_a[:, :], row_a[:, :]), deps=[h2])
            h4 = b.op("scalar", lambda s: s.activation(row_a[:, :], row_a[:, :], AF.Sqrt),
                      deps=[h3])
            hrb = emit_bcast_row(row_a[:, :], 1, [h4])
            free["rows_free"] = [hrb]
            gn_done = []
            for k in range(16):
                hgn = b.op("vector", lambda v, _k=k: v.scalar_tensor_tensor(
                    g_sb[:, _k, :], g_sb[:, _k, :], normw[:, _k:_k + 1], pb[1][:, :],
                    OP.mult, OP.mult), deps=ps.rdeps(1) + [ld0[5], free[f"sq{k % 2}"]])
                ps.read(1, hgn)
                gn_done.append(hgn)
            for mi in range(8):
                gmi = tci * 8 + mi
                hwl = b.dma("gpsimd", fview(woutbuf[gmi % 2][:, :, :], BF16), wout_src(mi),
                            deps=[free.get(f"woutb{gmi % 2}")])
                bank = 2 + (mi % 2)
                hmm = None
                dmm = ps.wdeps(bank) + [hwl] + gn_done
                for k in range(16):
                    hmm = b.op("tensor", lambda t, _mi=mi, _k=k, _b=bank, _g=gmi: t.matmul(
                        pb[_b][:, :], fview(woutbuf[_g % 2][:, _k, :], BF16), g_sb[:, _k, :],
                        start=(_k == 0), stop=(_k == 15)),
                        deps=dmm if k == 0 else [], inc=(k == 15))
                ps.wrote(bank, hmm)
                free[f"woutb{gmi % 2}"] = hmm
                free["g_free"] = hmm
                hx1 = b.op("vector", lambda v, _mi=mi, _b=bank, _t0=t0: v.tensor_tensor(
                    x_fm[:, _mi, _t0:_t0 + TC], x_fm[:, _mi, _t0:_t0 + TC], pb[_b][:, :],
                    OP.add), deps=ps.rdeps(bank))
                ps.read(bank, hx1)
                free[f"x1w{tci}_{mi}"] = hx1

        if PHASE_B:
            BAR = [(en, b.cnt[en]) for en in ENGS if b.cnt[en] > 0]
            TB = 1024
            NTB = T // TB
            xf = ab.view(o_sz, [128, NKC, T])
            hbit = x_fm[:, :, :].bitcast(BF16).rearrange("p a (b c) -> p (a b) c", b=4)
            x1buf = x_fm[:, 4:8, :]
            absscr = ab.view(o_g, [128, NKC, TC])
            sq2scr = ab.view(o_g + 4096, [128, NKC, TC])
            w1buf = [ab.view(o_g + i * 2048, [128, NKC, 256]) for i in range(4)]
            acc = ab.view(o_wout, [128, NKC, TB])
            tscr = ab.view(o_sqscr, [128, TB])
            wbc = ab.view(o_cacc, [128, TB])
            sel8_sb = ab.alloc([8, 8, 128])
            gw3_sb = ab.view(o_vbf, [128, 4, 8])
            _af_saved = af.off
            af.off = 0
            cmpscr = af.alloc([128, 128])
            eb = {}
            eb["e0b1"] = af.alloc([128, 3, 32]); eb["e0b2"] = af.alloc([128, 3, 8])
            eb["e1b1"] = af.alloc([128, 3, 16]); eb["e1b2"] = af.alloc([128, 3, 16])
            eb["e1b3"] = af.alloc([128, 3, 8]); eb["e2b1"] = af.alloc([128, 2, 24])
            eb["e2b2"] = af.alloc([128, 2, 8])
            affw_sb = af.alloc([128, NKC, 8])
            assert af.off <= 1440, af.off
            af.off = _af_saved
            hld = [b.dma("sync", sel8_sb, bass.AP(d_sel8, 0, [[8 * 128, 8], [128, 8], [1, 128]]), deps=BAR),
                   b.dma("gpsimd", gw3_sb, bass.AP(d_gw3, 0, [[8, 128], [128 * 8, 4], [1, 8]]), deps=BAR),
                   b.dma("sync", eb["e0b1"], d_e0b1[:, :, :], deps=BAR),
                   b.dma("sync", eb["e0b2"], d_e0b2[:, :, :], deps=BAR),
                   b.dma("sync", eb["e1b1"], d_e1b1[:, :, :], deps=BAR),
                   b.dma("sync", eb["e1b2"], d_e1b2[:, :, :], deps=BAR),
                   b.dma("sync", eb["e1b3"], d_e1b3[:, :, :], deps=BAR),
                   b.dma("sync", eb["e2b1"], d_e2b1[:, :, :], deps=BAR),
                   b.dma("sync", eb["e2b2"], d_e2b2[:, :, :], deps=BAR)]
            h_sel8 = hld[0]

            imp_dma = []
            l1_dma = []
            for tci in range(NTC):
                t0 = tci * TC
                bs_s = 0 if tci % 2 == 0 else 2
                bq_s = bs_s + 1
                hsumx = None
                for k in range(NKC):
                    hsumx = b.op("tensor", lambda t, _k=k, _t0=t0, _b=bs_s: t.matmul(
                        pb[_b][:1, :], ones_col[:, :], x_fm[:, _k, _t0:_t0 + TC],
                        start=(_k == 0), stop=(_k == NKC - 1)),
                        deps=(ps.wdeps(bs_s) + BAR + [h_ones]) if k == 0 else [], inc=True)
                ps.wrote(bs_s, hsumx)
                hsq2 = b.op("scalar", lambda s, _t0=t0, _p=tci % 2: s.activation(
                    sq2scr[:, :, :] if _p == 0 else absscr[:, :, :],
                    x_fm[:, :, _t0:_t0 + TC], AF.Square),
                    deps=BAR + [free.get(f"sq2free{tci % 2}"), free.get("absfree")])
                hsumq = None
                for k in range(NKC):
                    hsumq = b.op("tensor", lambda t, _k=k, _b=bq_s, _p=tci % 2: t.matmul(
                        pb[_b][:1, :], onescb,
                        (sq2scr if _p == 0 else absscr)[:, _k, :],
                        start=(_k == 0), stop=(_k == NKC - 1)),
                        deps=(ps.wdeps(bq_s) + [hsq2, h_onescb]) if k == 0 else [], inc=True)
                ps.wrote(bq_s, hsumq)
                free[f"sq2free{tci % 2}"] = hsumq
                h1 = b.op("vector", lambda v, _b=bs_s: v.tensor_scalar(
                    row_a[:, :], pb[_b][:1, :], LNW, None, OP.mult),
                    deps=ps.rdeps(bs_s) + [free.get("rows_free")])
                ps.read(bs_s, h1)
                h2 = b.op("vector", lambda v: v.tensor_mul(row_b[:, :], row_a[:, :], row_a[:, :]),
                          deps=[h1])
                h3 = b.op("vector", lambda v, _b=bq_s: v.scalar_tensor_tensor(
                    row_b[:, :], pb[_b][:1, :], LNW, row_b[:, :], OP.mult, OP.subtract),
                    deps=ps.rdeps(bq_s) + [h2])
                ps.read(bq_s, h3)
                hi1 = b.op("vector", lambda v: v.tensor_scalar_add(row_c[:, :], row_b[:, :], EPS),
                           deps=[h3, free.get("rows_free")])
                hi2 = b.op("vector", lambda v: v.reciprocal(row_c[:, :], row_c[:, :]), deps=[hi1])
                hi3 = b.op("vector", lambda v: v.tensor_mul(row_c[:, :], row_c[:, :], row_b[:, :]),
                           deps=[hi2])
                hi4 = b.op("vector", lambda v: v.tensor_scalar(row_c[:, :], row_c[:, :], float(DM),
                                                               None, OP.mult), deps=[hi3])
                hi5 = b.op("scalar", lambda s: s.activation(row_c[:, :], row_c[:, :], AF.Sqrt),
                           deps=[hi4])
                imp_dma.append(b.dma("sync", d_imp_in[0:1, t0:t0 + TC], row_c[:, :], deps=[hi5]))
                h4 = b.op("vector", lambda v: v.tensor_scalar_add(row_b[:, :], row_b[:, :], EPS),
                          deps=[h3, hi3])
                h5 = b.op("vector", lambda v: v.reciprocal(row_b[:, :], row_b[:, :]), deps=[h4])
                h6 = b.op("scalar", lambda s: s.activation(row_b[:, :], row_b[:, :], AF.Sqrt),
                          deps=[h5])
                h7 = b.op("vector", lambda v: v.tensor_scalar(
                    row_c[:, :], row_a[:, :], -1.0, None, OP.mult), deps=[h1, imp_dma[-1]])
                h8 = b.op("vector", lambda v: v.tensor_mul(row_c[:, :], row_c[:, :], row_b[:, :]),
                          deps=[h6, h7])
                hr = emit_bcast_row(row_b[:, :], bs_s, [h6])
                hn = emit_bcast_row(row_c[:, :], bq_s, [h8])
                free["rows_free"] = [hr, hn]
                ln2_done = []
                for k in range(NKC):
                    ha = b.op("vector", lambda v, _k=k, _t0=t0, _b=bs_s: v.tensor_tensor(
                        xf[:, _k, _t0:_t0 + TC], x_fm[:, _k, _t0:_t0 + TC], pb[_b][:, :],
                        OP.mult), deps=ps.rdeps(bs_s) + BAR)
                    ps.read(bs_s, ha)
                    hb2 = b.op("vector", lambda v, _k=k, _t0=t0, _b=bq_s: v.tensor_tensor(
                        xf[:, _k, _t0:_t0 + TC], xf[:, _k, _t0:_t0 + TC], pb[_b][:, :], OP.add),
                        deps=ps.rdeps(bq_s) + [ha])
                    ps.read(bq_s, hb2)
                    ln2_done.append(hb2)
                habs = b.op("scalar", lambda s, _t0=t0: s.activation(
                    absscr[:, :, :], xf[:, :, _t0:_t0 + TC], AF.Abs),
                    deps=ln2_done + [free.get("absfree"), free.get("sq2free1")])
                bl1 = 4 + tci % 2
                hl1 = None
                for k in range(NKC):
                    hl1 = b.op("tensor", lambda t, _k=k, _b=bl1: t.matmul(
                        pb[_b][:1, :], onescb, absscr[:, _k, :],
                        start=(_k == 0), stop=(_k == NKC - 1)),
                        deps=([habs] + ps.wdeps(bl1)) if k == 0 else [], inc=True)
                ps.wrote(bl1, hl1)
                free["absfree"] = hl1
                hl2 = b.op("vector", lambda v, _b=bl1: v.tensor_copy(row_a[:, :], pb[_b][:1, :]),
                           deps=ps.rdeps(bl1))
                ps.read(bl1, hl2)
                l1_dma.append(b.dma("sync", d_l1[0:1, t0:t0 + TC], row_a[:, :], deps=[hl2]))
                free["rows_free"] = free["rows_free"] + [hl2, l1_dma[-1]]

            h_spill = b.dma("sync", d_x1[:, :],
                            x_fm[:, :, :].rearrange("p a b -> p (a b)"),
                            deps=BAR + imp_dma + l1_dma + free["rows_free"])
            for m in range(32):
                free[f"hex{m}"] = [h_spill]
            hcc = b.op("gpsimd", lambda g: g.collective_compute(
                "AllGather", OP.bypass, replica_groups=[list(range(NCORES))],
                ins=[d_imp_in.ap().opt()], outs=[d_imp_out.ap().opt()]),
                deps=imp_dma)
            NR = (NCORES * T) // 128
            himp = b.dma("sync", impall[:, 0:NR],
                         bass.AP(d_imp_out, 0, [[NR, 128], [1, NR]]), deps=[hcc])

            NIMP = NCORES * T
            hb0m = b.op("vector", lambda v: v.memset(bis[:, 0:2], 0.0))
            hb1m = b.op("vector", lambda v: v.memset(bis[:, 2:4], 64.0), deps=[hb0m])
            hkt = ld0[16]
            hlast = [hb1m, himp, hkt]
            for it in range(45):
                ht1 = b.op("vector", lambda v: v.tensor_add(bis[:, 4:6], bis[:, 0:2], bis[:, 2:4]),
                           deps=hlast)
                ht2 = b.op("vector", lambda v: v.tensor_scalar(bis[:, 4:6], bis[:, 4:6], 0.5,
                                                               None, OP.mult), deps=[ht1])
                hcnt = []
                for j in range(2):
                    hc1 = b.op("vector", lambda v, _j=j: v.tensor_scalar(
                        cmpscr[:, 0:NR], impall[:, 0:NR], bis[:, 4 + _j:5 + _j], None,
                        OP.is_le), deps=[ht2, himp])
                    hc2 = b.op("vector", lambda v, _j=j: v.tensor_reduce(
                        bis[:, 6 + _j:7 + _j], cmpscr[:, 0:NR], AX.X, OP.add), deps=[hc1])
                    hcnt.append(hc2)
                hmm2 = b.op("tensor", lambda t: t.matmul(
                    pb[0][:, 0:2], ones_sq[:, :], bis[:, 6:8], start=True, stop=True),
                    deps=ps.wdeps(0) + hcnt + [h_onesq])
                ps.wrote(0, hmm2)
                hpred = b.op("vector", lambda v: v.tensor_tensor(
                    bis[:, 8:10], pb[0][:, 0:2], ktarg[:, :], OP.is_ge),
                    deps=ps.rdeps(0) + [hkt])
                ps.read(0, hpred)
                hd1 = b.op("vector", lambda v: v.tensor_sub(bis[:, 10:12], bis[:, 4:6], bis[:, 2:4]),
                           deps=[hpred])
                hd2 = b.op("vector", lambda v: v.tensor_mul(bis[:, 10:12], bis[:, 10:12],
                                                            bis[:, 8:10]), deps=[hd1])
                hd3 = b.op("vector", lambda v: v.tensor_add(bis[:, 2:4], bis[:, 2:4],
                                                            bis[:, 10:12]), deps=[hd2])
                hn1 = b.op("vector", lambda v: v.tensor_scalar(bis[:, 12:14], bis[:, 8:10], -1.0,
                                                               1.0, OP.mult, OP.add), deps=[hpred])
                hn2 = b.op("vector", lambda v: v.tensor_sub(bis[:, 10:12], bis[:, 4:6],
                                                            bis[:, 0:2]), deps=[hd2])
                hn3 = b.op("vector", lambda v: v.tensor_mul(bis[:, 10:12], bis[:, 10:12],
                                                            bis[:, 12:14]), deps=[hn1, hn2])
                hn4 = b.op("vector", lambda v: v.tensor_add(bis[:, 0:2], bis[:, 0:2],
                                                            bis[:, 10:12]), deps=[hn3])
                hlast = [hd3, hn4]
            hth1 = b.op("vector", lambda v: v.tensor_scalar(
                bis[:, 14:15], bis[:, 2:3], float(_PCT[0]), None, OP.mult), deps=hlast)
            hthr = b.op("vector", lambda v: v.scalar_tensor_tensor(
                bis[:, 14:15], bis[:, 3:4], float(_PCT[1]), bis[:, 14:15], OP.mult, OP.add),
                deps=[hth1])
            hs1 = b.op("vector", lambda v: v.tensor_reduce(bis[:, 6:7], impall[:, 0:NR],
                                                           AX.X, OP.add),
                       deps=[himp] + hlast)
            hs2 = b.op("tensor", lambda t: t.matmul(pb[0][:, 0:1], ones_sq[:, :], bis[:, 6:7],
                                                    start=True, stop=True),
                       deps=ps.wdeps(0) + [hs1])
            ps.wrote(0, hs2)
            hs3 = b.op("vector", lambda v: v.tensor_scalar(bis[:, 6:7], pb[0][:, 0:1],
                                                           1.0 / NIMP, None, OP.mult),
                       deps=ps.rdeps(0))
            ps.read(0, hs3)
            hs4 = b.op("vector", lambda v: v.tensor_scalar(cmpscr[:, 0:NR], impall[:, 0:NR],
                                                           bis[:, 6:7], None, OP.subtract),
                       deps=[hs3])
            hs5 = b.op("vector", lambda v: v.tensor_mul(cmpscr[:, 0:NR], cmpscr[:, 0:NR],
                                                        cmpscr[:, 0:NR]), deps=[hs4])
            hs6 = b.op("vector", lambda v: v.tensor_reduce(bis[:, 6:7], cmpscr[:, 0:NR],
                                                           AX.X, OP.add), deps=[hs5])
            hs7 = b.op("tensor", lambda t: t.matmul(pb[0][:, 0:1], ones_sq[:, :], bis[:, 6:7],
                                                    start=True, stop=True),
                       deps=ps.wdeps(0) + [hs6])
            ps.wrote(0, hs7)
            hs8 = b.op("vector", lambda v: v.tensor_scalar(bis[:, 15:16], pb[0][:, 0:1],
                                                           1.0 / NIMP, None, OP.mult),
                       deps=ps.rdeps(0))
            ps.read(0, hs8)
            hs9 = b.op("scalar", lambda s: s.activation(bis[:, 15:16], bis[:, 15:16], AF.Sqrt),
                       deps=[hs8])
            htmp = b.op("vector", lambda v: v.tensor_scalar(bis[:, 15:16], bis[:, 15:16], 0.1,
                                                            1.0, OP.mult, OP.add), deps=[hs9])

            xf_done = []
            for tci in range(NTC):
                t0 = tci * TC
                hia = b.dma("sync", row_a[:, :], d_imp_in[0:1, t0:t0 + TC],
                            deps=[hthr, imp_dma[tci], free.get("rows_free")])
                hib = b.dma("sync", row_b[:, :], d_l1[0:1, t0:t0 + TC],
                            deps=[htmp, l1_dma[tci], free.get("rows_free")])
                hm1 = b.op("vector", lambda v: v.tensor_scalar(
                    row_c[:, :], row_a[:, :], bis[0:1, 14:15], None, OP.is_gt),
                    deps=[hia, hthr, free.get("rows_free")])
                hm2 = b.op("vector", lambda v: v.tensor_scalar_add(row_b[:, :], row_b[:, :], 1e-8),
                           deps=[hib])
                hm3 = b.op("vector", lambda v: v.tensor_scalar(
                    row_b[:, :], row_b[:, :], bis[0:1, 15:16], None, OP.mult),
                    deps=[hm2, htmp])
                hm4 = b.op("vector", lambda v: v.reciprocal(row_b[:, :], row_b[:, :]), deps=[hm3])
                hm5 = b.op("vector", lambda v: v.tensor_mul(row_c[:, :], row_c[:, :], row_b[:, :]),
                           deps=[hm1, hm4])
                hsb = emit_bcast_row(row_c[:, :], 0, [hm5])
                free["rows_free"] = [hsb]
                for k in range(NKC):
                    hx = b.op("vector", lambda v, _k=k, _t0=t0: v.tensor_tensor(
                        xf[:, _k, _t0:_t0 + TC], xf[:, _k, _t0:_t0 + TC], pb[0][:, :], OP.mult),
                        deps=ps.rdeps(0))
                    ps.read(0, hx)
                    xf_done.append(hx)

            hxm = []
            for k in range(NKC):
                hxm.append(b.op("vector", lambda v, _k=k: v.tensor_reduce(
                    bis[:, 8 + _k:9 + _k], xf[:, _k, :], AX.X, OP.add), deps=xf_done))
            haw = b.dma("sync", affw_sb, bass.AP(d_affw, 0, [[8, 128], [128 * 8, NKC], [1, 8]]),
                        deps=BAR)
            haf = None
            for k in range(NKC):
                haf = b.op("tensor", lambda t, _k=k: t.matmul(
                    pb[1][:8, 0:1], affw_sb[:, _k, :], bis[:, 8 + _k:9 + _k],
                    start=(_k == 0), stop=(_k == NKC - 1)),
                    deps=(ps.wdeps(1) + [haw] + hxm) if k == 0 else [], inc=True)
            ps.wrote(1, haf)
            ha2 = b.op("vector", lambda v: v.scalar_tensor_tensor(
                affb[:, :], pb[1][:8, 0:1], 1.0 / T, affb[:, :], OP.mult, OP.add),
                deps=ps.rdeps(1) + [ld0[14]])
            ps.read(1, ha2)
            ha3 = b.op("vector", lambda v: v.tensor_scalar(affb[:, :], affb[:, :], 0.3,
                                                           None, OP.mult), deps=[ha2])
            ha4 = b.op("tensor", lambda t: t.transpose(pb[1][:1, 0:8], affb[:, :],
                                                       ident[:8, :8]),
                       deps=ps.wdeps(1) + [ha3, ld0[0]])
            ps.wrote(1, ha4)
            ha5 = b.op("vector", lambda v: v.tensor_copy(scl_sb[0:1, 0:8], pb[1][:1, 0:8]),
                       deps=ps.rdeps(1))
            ps.read(1, ha5)
            ha6 = b.op("tensor", lambda t: t.matmul(pb[1][:, 0:8], ones_row[:, :],
                                                    scl_sb[0:1, 0:8], start=True, stop=True),
                       deps=ps.wdeps(1) + [ha5, h_onesrow])
            ps.wrote(1, ha6)
            ha7 = b.op("vector", lambda v: v.tensor_copy(affbc[:, :], pb[1][:, 0:8]),
                       deps=ps.rdeps(1))
            ps.read(1, ha7)

            he0 = b.op("vector", lambda v: v.tensor_scalar_add(nsc[:8, 0:1], eu_sb[:, :], 1e-8),
                       deps=[ld0[15]])
            he1 = b.op("scalar", lambda s: s.activation(nsc[:8, 0:1], nsc[:8, 0:1], AF.Ln),
                       deps=[he0])
            he2 = b.op("vector", lambda v: v.tensor_mul(nsc[:8, 0:1], nsc[:8, 0:1], eu_sb[:, :]),
                       deps=[he1])
            he3 = b.op("tensor", lambda t: t.matmul(pb[2][:1, 0:1], ones_col[:8, :],
                                                    nsc[:8, 0:1], start=True, stop=True),
                       deps=ps.wdeps(2) + [he2, h_ones])
            ps.wrote(2, he3)
            he4 = b.op("vector", lambda v: v.tensor_scalar(scl_sb[0:1, 1:2], pb[2][:1, 0:1],
                                                           0.01, 0.02, OP.mult, OP.add),
                       deps=ps.rdeps(2))
            ps.read(2, he4)
            he5 = b.op("vector", lambda v: v.tensor_scalar(scl_sb[0:1, 1:2], scl_sb[0:1, 1:2],
                                                           1e-4, None, OP.max), deps=[he4])
            he6 = b.op("tensor", lambda t: t.matmul(pb[2][:, 0:1], ones_row[:, :],
                                                    scl_sb[0:1, 1:2], start=True, stop=True),
                       deps=ps.wdeps(2) + [he5, h_onesrow])
            ps.wrote(2, he6)
            he7 = b.op("vector", lambda v: v.tensor_copy(nsc[:, :], pb[2][:, 0:1]),
                       deps=ps.rdeps(2))
            ps.read(2, he7)

            def stream_pass(wsrc, nm, nkg, rhs_slices, evac, extra_dep):
                li = [0]
                out_h = []
                for mp in range(nm // 2):
                    hload = {}
                    hmm_last = {0: None, 1: None}
                    for kg in range(nkg):
                        par = li[0] % 4
                        hload[kg] = b.dma("gpsimd", w1buf[par], wsrc(kg, mp),
                                          deps=[free.get(f"w1b{par}")] + list(ld_dep))
                        for mi2 in (0, 1):
                            mi = 2 * mp + mi2
                            bank = [2, 3, 4, 5][mi % 4]
                            hmm = None
                            for k in range(8):
                                kk = kg * 8 + k
                                first = (kg == 0 and k == 0)
                                last = (kg == nkg - 1 and k == 7)
                                dmm = []
                                if first:
                                    dmm = ps.wdeps(bank) + list(extra_dep)
                                if k == 0:
                                    dmm = dmm + [hload[kg]]
                                hmm = b.op("tensor",
                                           lambda t, _p=par, _k=k, _m=mi2, _b=bank,
                                                  _r=rhs_slices[kk], _f=first, _l=last:
                                           t.matmul(pb[_b][:, :],
                                                    w1buf[_p][:, _k, 128 * _m:128 * (_m + 1)],
                                                    _r, start=_f, stop=_l),
                                           deps=dmm, inc=(k == 7))
                            hmm_last[mi2] = hmm
                        free[f"w1b{par}"] = [hmm_last[0], hmm_last[1]]
                        li[0] += 1
                    for mi2 in (0, 1):
                        mi = 2 * mp + mi2
                        bank = [2, 3, 4, 5][mi % 4]
                        ps.wrote(bank, hmm_last[mi2])
                        out_h.append(evac(mi, bank, hmm_last[mi2]))
                return out_h

            def stream_pass(wsrc, nm, nkg, rhs_fn, evac, extra_dep, ld_dep=()):
                li = [0]
                out_h = []
                for mp in range(nm // 2):
                    hload = {}
                    hlast = {}
                    for kg in range(nkg):
                        par = li[0] % 4
                        hload[kg] = b.dma("gpsimd", w1buf[par], wsrc(kg, mp),
                                          deps=[free.get(f"w1b{par}")] + list(ld_dep))
                        for mi2 in (0, 1):
                            mi = 2 * mp + mi2
                            pair = (2, 3) if mi % 2 == 0 else (4, 5)
                            for half in (0, 1):
                                bank = pair[half]
                                hmm = None
                                for k in range(8):
                                    kk = kg * 8 + k
                                    first = (kg == 0 and k == 0)
                                    last = (kg == nkg - 1 and k == 7)
                                    dmm = []
                                    if first:
                                        dmm = ps.wdeps(bank) + list(extra_dep)
                                    if k == 0:
                                        dmm = dmm + [hload[kg]]
                                    hmm = b.op("tensor",
                                               lambda t, _p=par, _k=k, _m=mi2, _b=bank,
                                                      _r=rhs_fn(kk, half), _f=first, _l=last:
                                               t.matmul(pb[_b][:, :],
                                                        w1buf[_p][:, _k, 128 * _m:128 * (_m + 1)],
                                                        _r, start=_f, stop=_l),
                                               deps=dmm, inc=(k == 7))
                                hlast[(mi2, half)] = hmm
                        free[f"w1b{par}"] = [hlast[(0, 0)], hlast[(0, 1)],
                                             hlast[(1, 0)], hlast[(1, 1)]]
                        li[0] += 1
                    for mi2 in (0, 1):
                        mi = 2 * mp + mi2
                        pair = (2, 3) if mi % 2 == 0 else (4, 5)
                        ps.wrote(pair[0], hlast[(mi2, 0)])
                        ps.wrote(pair[1], hlast[(mi2, 1)])
                        out_h.extend(evac(mi, pair, (hlast[(mi2, 0)], hlast[(mi2, 1)])))
                return out_h

            PHYS = [("e0", 0), ("e0", 1), ("e0", 2),
                    ("e1", 0), ("e1", 1), ("e1", 2),
                    ("e2", 0), ("e2", 1)]
            SLOT = [0, 3, 6, 1, 4, 7, 2, 5]
            out_dmas = []
            gsq = ab.view(o_wout, [128, 8, TC])
            for tcb in range(NTB):
                t0 = tcb * TB
                xf_rhs = lambda kk, half, _t0=t0: xf[:, kk, _t0 + 512 * half:_t0 + 512 * (half + 1)]

                def ev_gateh(mi, pair, hl):
                    hs2 = []
                    for half in (0, 1):
                        hv = b.op("scalar", lambda s, _m=mi, _b=pair[half], _h=half:
                                  s.activation(hbit[:, _m, 512 * _h:512 * (_h + 1)],
                                               pb[_b][:, :], AF.Silu,
                                               bias=gb1[:, _m:_m + 1]),
                                  deps=ps.rdeps(pair[half]) + [ld0[11],
                                                               free.get(f"hex{mi}")])
                        ps.read(pair[half], hv)
                        hs2.append(hv)
                    free[f"hex{mi}"] = hs2
                    return hs2
                gh = stream_pass(
                    lambda kg, mp: bass.AP(d_gw1, 256 * mp,
                                           [[DM, 128], [128 * DM, 8], [1, 256]]),
                    8, 1, xf_rhs, ev_gateh, xf_done,
                    ld_dep=[free.get("absfree"), free.get("sq2free0"),
                            free.get("sq2free1")] if tcb == 0 else ())
                hnrm = []
                for blk in range(2):
                    bs = 512 * blk
                    hsumh = None
                    for k in range(8):
                        hsumh = b.op("tensor", lambda t, _k=k, _s=bs: t.matmul(
                            pb[0][:1, :], onescb, hbit[:, _k, _s:_s + 512],
                            start=(_k == 0), stop=(_k == 7)),
                            deps=(ps.wdeps(0) + gh + [h_onescb]) if k == 0 else [],
                            inc=True)
                    ps.wrote(0, hsumh)
                    hsqh = b.op("scalar", lambda s, _s=bs: s.activation(
                        gsq[:, :, :], hbit[:, 0:8, _s:_s + 512], AF.Square),
                        deps=gh + [free.get("gsqfree")])
                    hsumq = None
                    for k in range(8):
                        hsumq = b.op("tensor", lambda t, _k=k: t.matmul(
                            pb[1][:1, :], onescb, gsq[:, _k, :],
                            start=(_k == 0), stop=(_k == 7)),
                            deps=(ps.wdeps(1) + [hsqh, h_onescb]) if k == 0 else [],
                            inc=True)
                    ps.wrote(1, hsumq)
                    free["gsqfree"] = hsumq
                    hrows2 = emit_ln_rows(0, 1, DM, EPS)
                    hr2 = emit_bcast_row(row_b[:, :], 0, [hrows2[0]])
                    hn2 = emit_bcast_row(row_c[:, :], 1, [hrows2[1]])
                    free["rows_free"] = [hr2, hn2]
                    for k in range(8):
                        hq1 = b.op("vector", lambda v, _k=k, _s=bs: v.tensor_tensor(
                            hbit[:, _k, _s:_s + 512], hbit[:, _k, _s:_s + 512],
                            pb[0][:, :], OP.mult), deps=ps.rdeps(0) + gh)
                        ps.read(0, hq1)
                        hq2 = b.op("vector", lambda v, _k=k, _s=bs: v.tensor_tensor(
                            hbit[:, _k, _s:_s + 512], hbit[:, _k, _s:_s + 512],
                            pb[1][:, :], OP.add), deps=ps.rdeps(1) + [hq1])
                        ps.read(1, hq2)
                        hq3 = b.op("vector", lambda v, _k=k, _s=bs: v.tensor_scalar(
                            hbit[:, _k, _s:_s + 512], hbit[:, _k, _s:_s + 512],
                            lnsc[:, _k:_k + 1], lnof[:, _k:_k + 1], OP.mult, OP.add),
                            deps=[hq2, ld0[9], ld0[10]])
                        hnrm.append(hq3)
                for k in range(8):
                    free[f"hex{k}"] = hnrm
                hn_rhs = lambda kk, half: hbit[:, kk, 512 * half:512 * (half + 1)]
                def ev_gateh2(mi, pair, hl):
                    hs2 = []
                    for half in (0, 1):
                        hv = b.op("scalar", lambda s, _m=mi, _b=pair[half], _h=half:
                                  s.activation(hbit[:, 8 + _m, 512 * _h:512 * (_h + 1)],
                                               pb[_b][:, :], AF.Silu,
                                               bias=gb2[:, _m:_m + 1]),
                                  deps=ps.rdeps(pair[half]) + [ld0[12],
                                                               free.get(f"hex{8 + mi}")])
                        ps.read(pair[half], hv)
                        hs2.append(hv)
                    free[f"hex{8 + mi}"] = hs2
                    return hs2
                g2 = stream_pass(
                    lambda kg, mp: bass.AP(d_gw2, 256 * mp,
                                           [[512, 128], [128 * 512, 8], [1, 256]]),
                    4, 1, hn_rhs, ev_gateh2, hnrm)
                wrow_h = []
                for tt in range(8):
                    gt = tcb * 8 + tt
                    ts0 = t0 + tt * 128
                    hlg = None
                    for k in range(4):
                        hlg = b.op("tensor", lambda t, _k=k, _s=ts0 - t0: t.matmul(
                            pb[1][:, 0:8], hbit[:, 8 + _k, _s:_s + 128],
                            gw3_sb[:, _k, :], start=(_k == 0), stop=(_k == 3)),
                            deps=(ps.wdeps(1) + g2 + [hld[1]]) if k == 0 else [],
                            inc=(k == 3))
                    ps.wrote(1, hlg)
                    C = cmpscr
                    hv1 = b.op("vector", lambda v: v.tensor_tensor(
                        C[:, 0:8], pb[1][:, 0:8], gb3bc[:, :], OP.add),
                        deps=ps.rdeps(1) + [h_gb3, free.get("cmpfree")])
                    ps.read(1, hv1)
                    hv2 = b.op("vector", lambda v: v.tensor_tensor(
                        C[:, 0:8], C[:, 0:8], affbc[:, :], OP.add), deps=[hv1, ha7])
                    hv3 = b.op("vector", lambda v, _g=gt: v.scalar_tensor_tensor(
                        C[:, 0:8], noise_sb[:, _g, :], nsc[:, :], C[:, 0:8],
                        OP.mult, OP.add), deps=[hv2, he7, ld0[17]])
                    hv4 = b.op("vector", lambda v: v.tensor_scalar(
                        C[:, 0:8], C[:, 0:8], 1.0 / 1.5, None, OP.mult), deps=[hv3])
                    hv5 = b.op("vector", lambda v: v.tensor_reduce(
                        C[:, 16:17], C[:, 0:8], AX.X, OP.max), deps=[hv4])
                    hv6 = b.op("vector", lambda v: v.tensor_scalar(
                        C[:, 24:32], C[:, 0:8], C[:, 16:17], None, OP.is_equal), deps=[hv5])
                    hv7 = b.op("vector", lambda v: v.scalar_tensor_tensor(
                        C[:, 32:40], C[:, 24:32], -1e30, C[:, 0:8], OP.mult, OP.add),
                        deps=[hv6])
                    hv8 = b.op("vector", lambda v: v.tensor_reduce(
                        C[:, 17:18], C[:, 32:40], AX.X, OP.max), deps=[hv7])
                    hv9 = b.op("vector", lambda v: v.tensor_scalar(
                        C[:, 24:32], C[:, 0:8], C[:, 17:18], None, OP.is_ge), deps=[hv8])
                    hva = b.op("vector", lambda v: v.tensor_scalar(
                        C[:, 32:40], C[:, 0:8], C[:, 16:17], None, OP.subtract), deps=[hv9])
                    hvb = b.op("scalar", lambda s: s.activation(
                        C[:, 32:40], C[:, 32:40], AF.Exp), deps=[hva])
                    hvc = b.op("vector", lambda v: v.tensor_mul(
                        C[:, 40:48], C[:, 32:40], C[:, 24:32]), deps=[hvb])
                    hvd = b.op("vector", lambda v: v.tensor_reduce(
                        C[:, 18:19], C[:, 40:48], AX.X, OP.add), deps=[hvc])
                    hve = b.op("vector", lambda v: v.reciprocal(C[:, 19:20], C[:, 18:19]),
                               deps=[hvd])
                    hvf = b.op("vector", lambda v: v.tensor_scalar(
                        C[:, 0:8], C[:, 40:48], C[:, 19:20], None, OP.mult), deps=[hve])
                    hvt = b.op("tensor", lambda t: t.transpose(
                        pb[1][:8, 128:256], C[:, 0:8], ident[:128, :128]),
                        deps=ps.wdeps(1) + [hvf, ld0[0]])
                    ps.wrote(1, hvt)
                    hvw = b.op("vector", lambda v, _g=gt: v.tensor_copy(
                        wrows[:, 128 * _g:128 * (_g + 1)], pb[1][:8, 128:256]),
                        deps=ps.rdeps(1) + [free.get("wrows_free")])
                    ps.read(1, hvw)
                    free["cmpfree"] = hvf
                    wrow_h.append(hvw)

                for p, (et, ei) in enumerate(PHYS):
                    slot = SLOT[p]
                    hwb = []
                    for half in (0, 1):
                        bank = 6 + half
                        h1w = b.op("tensor", lambda t, _s=slot, _t0=t0, _h=half, _b=bank:
                                   t.matmul(pb[_b][:, :], sel8_sb[:, _s, :],
                                            wrows[:, _t0 + 512 * _h:_t0 + 512 * (_h + 1)],
                                            start=True, stop=True),
                                   deps=ps.wdeps(bank) + wrow_h + [h_sel8])
                        ps.wrote(bank, h1w)
                        h2w = b.op("vector", lambda v, _h=half, _b=bank: v.tensor_copy(
                            wbc[:, 512 * _h:512 * (_h + 1)], pb[_b][:, :]),
                            deps=ps.rdeps(bank) + [free.get("wbcfree")])
                        ps.read(bank, h2w)
                        hwb.append(h2w)
                    if et == "e0":
                        nmh, nkg_o, wh, wo, b1t, b2t = 32, 4, d_e0w1, d_e0w2, eb["e0b1"], eb["e0b2"]
                        hsz, act = 4096, AF.Gelu_apprx_tanh
                    elif et == "e1":
                        nmh, nkg_o, wh, wo, b1t, b2t = 16, 2, d_e1w1, d_e1w3, eb["e1b1"], eb["e1b3"]
                        hsz, act = DIN, AF.Silu
                    else:
                        nmh, nkg_o, wh, wo, b1t, b2t = 24, 3, d_e2w1, d_e2w2, eb["e2b1"], eb["e2b2"]
                        hsz, act = 3072, AF.Sigmoid
                    def ev_h(mi, pair, hl, _act=act, _b1=b1t, _ei=ei, _et=et):
                        hs2 = []
                        for half in (0, 1):
                            bank = pair[half]
                            if _et != "e2":
                                hv = b.op("scalar", lambda s, _m=mi, _b=bank, _h=half:
                                          s.activation(hbit[:, _m, 512 * _h:512 * (_h + 1)],
                                                       pb[_b][:, :], _act,
                                                       bias=_b1[:, _ei, _m:_m + 1]),
                                          deps=ps.rdeps(bank) + hld +
                                               [free.get(f"hex{mi}")])
                                ps.read(bank, hv)
                                hs2.append(hv)
                                continue
                            hv = b.op("scalar", lambda s, _m=mi, _b=bank, _h=half:
                                      s.activation(hbit[:, _m, 512 * _h:512 * (_h + 1)],
                                                   pb[_b][:, :], AF.Sigmoid,
                                                   bias=_b1[:, _ei, _m:_m + 1]),
                                      deps=ps.rdeps(bank) + hld + [free.get(f"hex{mi}")])
                            ps.read(bank, hv)
                            hv2 = b.op("vector", lambda v, _m=mi, _b=bank, _c=_b1,
                                       _e=_ei, _h=half: v.scalar_tensor_tensor(
                                           tscr[:, 512 * _h:512 * (_h + 1)], pb[_b][:, :],
                                           _c[:, _e, _m:_m + 1],
                                           hbit[:, _m, 512 * _h:512 * (_h + 1)],
                                           OP.add, OP.mult),
                                       deps=ps.rdeps(bank) + [hv, free.get("tscrfree")])
                            ps.read(bank, hv2)
                            hv3 = b.op("vector", lambda v, _m=mi, _h=half: v.tensor_mul(
                                hbit[:, _m, 512 * _h:512 * (_h + 1)],
                                tscr[:, 512 * _h:512 * (_h + 1)],
                                hbit[:, _m, 512 * _h:512 * (_h + 1)]), deps=[hv2])
                            free["tscrfree"] = hv3
                            hs2.append(hv3)
                        free[f"hex{mi}"] = hs2
                        return hs2
                    def wsrc_h(kg, mp, _w=wh, _ei=ei, _hs=hsz):
                        return bass.AP(_w, _ei * DM * _hs + 256 * mp,
                                       [[_hs, 128], [128 * _hs, 8], [1, 256]])
                    hh = stream_pass(wsrc_h, nmh, 1, xf_rhs, ev_h, xf_done + hwb)
                    if et == "e1":
                        def ev_h1b(mi, pair, hl, _ei=ei):
                            hs2 = []
                            for half in (0, 1):
                                bank = pair[half]
                                hv = b.op("scalar", lambda s, _m=mi, _b=bank, _h=half:
                                          s.activation(
                                              hbit[:, 16 + _m, 512 * _h:512 * (_h + 1)],
                                              pb[_b][:, :], AF.Silu,
                                              bias=eb["e1b2"][:, _ei, _m:_m + 1]),
                                          deps=ps.rdeps(bank) + hld +
                                               [free.get(f"hex{16 + mi}")])
                                ps.read(bank, hv)
                                hs2.append(hv)
                            free[f"hex{16 + mi}"] = hs2
                            return hs2
                        def wsrc_h1b(kg, mp, _ei=ei):
                            return bass.AP(d_e1w2,
                                           _ei * DIN * DIN + kg * 8 * 128 * DIN + 256 * mp,
                                           [[DIN, 128], [128 * DIN, 8], [1, 256]])
                        hh2 = stream_pass(wsrc_h1b, 16, 2, hn_rhs, ev_h1b, hh)
                        out_rhs = lambda kk, half: hbit[:, 16 + kk, 512 * half:512 * (half + 1)]
                        out_dep = hh2
                    elif et == "e0":
                        out_rhs = hn_rhs
                        out_dep = hh
                    else:
                        out_rhs = hn_rhs
                        out_dep = hh
                    first_e = (p == 0)
                    def ev_out(mi, pair, hl, _b2=b2t, _ei=ei, _fe=first_e):
                        hs2 = []
                        for half in (0, 1):
                            bank = pair[half]
                            if _fe:
                                hc = b.op("vector", lambda v, _m=mi, _b=bank, _h=half:
                                          v.scalar_tensor_tensor(
                                              acc[:, _m, 512 * _h:512 * (_h + 1)],
                                              pb[_b][:, :], _b2[:, _ei, _m:_m + 1],
                                              wbc[:, 512 * _h:512 * (_h + 1)],
                                              OP.add, OP.mult),
                                          deps=ps.rdeps(bank) + hld + hwb +
                                               [free.get(f"acc{mi}"), free.get("gsqfree")])
                                ps.read(bank, hc)
                            else:
                                hc1 = b.op("vector", lambda v, _m=mi, _b=bank, _h=half:
                                           v.scalar_tensor_tensor(
                                               tscr[:, 512 * _h:512 * (_h + 1)],
                                               pb[_b][:, :], _b2[:, _ei, _m:_m + 1],
                                               wbc[:, 512 * _h:512 * (_h + 1)],
                                               OP.add, OP.mult),
                                           deps=ps.rdeps(bank) + hld + hwb +
                                                [free.get("tscrfree")])
                                ps.read(bank, hc1)
                                hc = b.op("vector", lambda v, _m=mi, _h=half:
                                          v.tensor_tensor(
                                              acc[:, _m, 512 * _h:512 * (_h + 1)],
                                              acc[:, _m, 512 * _h:512 * (_h + 1)],
                                              tscr[:, 512 * _h:512 * (_h + 1)], OP.add),
                                          deps=[hc1, free.get(f"acc{mi}")])
                                free["tscrfree"] = hc
                            free[f"acc{mi}"] = hc
                            hs2.append(hc)
                        return hs2
                    def wsrc_o(kg, mp, _w=wo, _ei=ei, _hs=hsz):
                        return bass.AP(_w, _ei * _hs * DM + kg * 8 * 128 * DM + 256 * mp,
                                       [[DM, 128], [128 * DM, 8], [1, 256]])
                    ho = stream_pass(wsrc_o, 8, nkg_o, out_rhs, ev_out, out_dep + hwb)
                    free["wbcfree"] = ho
                free["wrows_free"] = free.get("wbcfree")
                x1v = x1buf[:, :, :].rearrange("p a (c b) -> p (a c) b", b=TB)
                hx1l = b.dma("sync", x1v,
                             bass.AP(d_x1, t0, [[NKC * T, 128], [2048, 8], [1, TB]]),
                             deps=[free.get("wbcfree"), free.get("x1buf_free")])
                hf1 = b.op("vector", lambda v: v.tensor_tensor(
                    x1v, x1v, acc[:, :, :], OP.add),
                    deps=[hx1l] + [free.get(f"acc{m}") for m in range(8)])
                hf2 = b.op("vector", lambda v, _t0=t0: v.scalar_tensor_tensor(
                    x1v, xf[:, :, _t0:_t0 + TB], 0.1, x1v, OP.mult, OP.add),
                    deps=[hf1] + xf_done)
                for m in range(8):
                    free[f"acc{m}"] = hf2
                free["gsqfree"] = hf2
                hdo = b.dma("sync",
                            bass.AP(d_out, t0, [[T, 128], [128 * T, NKC], [1, TB]]),
                            x1v, deps=[hf2])
                out_dmas.append(hdo)
                free["x1buf_free"] = hdo
                free["hex16plus"] = hdo
                for m in range(8, 32):
                    free[f"hex{m}"] = [free.get(f"hex{m}"), hdo]
            free["out_dmas"] = out_dmas

        if PHASE_B:
            b.op("sync", lambda s: None, deps=free["out_dmas"], inc=False)
        else:
            x1_done = [free[f"x1w{i}_{m}"] for i in range(NTC) for m in range(8)]
            h_out = b.dma("sync", bass.AP(d_out, 0, [[T, 128], [128 * T, NKC], [1, T]]),
                          x_fm[:, :, :], deps=x1_done)
            b.op("sync", lambda s: None, deps=[h_out], inc=False)

        b.emit(nc, block, sems, strict=STRICT_SYNC)
    return nc


PHASE_B = True
STRICT_SYNC = False
_PCT = (0.3, 0.7)


def _host_consts():
    tri = np.triu(np.ones((128, 128), dtype=np.float32))
    ident = np.eye(128, dtype=np.float32)
    sel = np.zeros((NH, 16 * 128), dtype=np.float32)
    for j in range(16):
        for p in range(128):
            sel[2 * j + p // 64, j * 128 + p] = 1.0
    sel = sel.astype(ml_dtypes.bfloat16)
    ktarg = np.zeros((128, 2), dtype=np.float32)
    ktarg[:, 0] = 14745.0
    ktarg[:, 1] = 14746.0
    return tri, ident, sel, ktarg


def _noise_const():
    import jax
    import jax.numpy as jnp
    cpu = jax.devices("cpu")[0]
    with jax.default_device(cpu):
        n = jax.random.normal(jax.random.key(1), (NCORES, T, 8), dtype=jnp.float32)
        return np.asarray(n)


def kernel(**inp):
    x = np.asarray(inp["L0_attn_x"], dtype=np.float32)
    tri, ident, sel, ktarg = _host_consts()
    n_imp = NCORES * T
    qq = np.float32(np.float32(90.0) / np.float32(100.0)) * np.float32(n_imp - 1)
    low = float(np.floor(qq))
    hw_ = np.float32(qq - np.float32(low))
    lw_ = np.float32(np.float32(1.0) - hw_)
    global _PCT
    _PCT = (float(lw_), float(hw_))
    ktarg = np.zeros((128, 2), dtype=np.float32)
    ktarg[:, 0] = low + 1
    ktarg[:, 1] = low + 2
    sel8 = np.zeros((8, 8, 128), dtype=np.float32)
    for s in range(8):
        sel8[s, s, :] = 1.0
    sel8 = sel8.reshape(8, 8 * 128).astype(ml_dtypes.bfloat16)
    noise = _noise_const()

    conv_w = np.asarray(inp["L0_attn_conv_w"])
    convw_t = np.ascontiguousarray(conv_w[::-1].T)
    convb_t = np.ascontiguousarray(
        np.asarray(inp["L0_attn_conv_b"]).reshape(18, 128).T)
    normw_t = np.ascontiguousarray(
        np.asarray(inp["L0_attn_norm_w"]).reshape(16, 128).T)
    d_rep = np.ascontiguousarray(
        np.repeat(np.asarray(inp["L0_attn_D"]), HD).reshape(16, 128).T)
    lnsc_rep = np.ascontiguousarray(
        np.asarray(inp["L0_ffn_ln_scale"]).reshape(8, 128).T)
    lnof_rep = np.ascontiguousarray(
        np.asarray(inp["L0_ffn_ln_offset"]).reshape(8, 128).T)
    gb1_rep = np.ascontiguousarray(
        np.asarray(inp["L0_ffn_gate_b1"]).reshape(8, 128).T)
    gb2_rep = np.ascontiguousarray(
        np.asarray(inp["L0_ffn_gate_b2"]).reshape(4, 128).T)

    shared = {
        "in_proj_w": np.ascontiguousarray(inp["L0_attn_in_proj_w"]),
        "conv_w_t": convw_t,
        "conv_b_t": convb_t,
        "norm_w_t": normw_t,
        "out_proj_w": np.ascontiguousarray(inp["L0_attn_out_proj_w"]),
        "dt_bias": np.asarray(inp["L0_attn_dt_bias"]).reshape(NH, 1),
        "A_log": np.asarray(inp["L0_attn_A_log"]).reshape(NH, 1),
        "D_rep": d_rep,
        "expert_usage": np.asarray(inp["L0_ffn_expert_usage"]).reshape(8, 1),
        "gate_w1": np.ascontiguousarray(inp["L0_ffn_gate_w1"]),
        "gate_b1_rep": gb1_rep,
        "ln_scale_rep": lnsc_rep,
        "ln_offset_rep": lnof_rep,
        "gate_w2": np.ascontiguousarray(inp["L0_ffn_gate_w2"]),
        "gate_b2_rep": gb2_rep,
        "gate_w3": np.ascontiguousarray(inp["L0_ffn_gate_w3"]),
        "gate_b3_row": np.asarray(inp["L0_ffn_gate_b3"]).reshape(1, 8),
        "aff_w": np.ascontiguousarray(inp["L0_ffn_aff_w"]),
        "aff_b": np.asarray(inp["L0_ffn_aff_b"]).reshape(8, 1),
        "e0_w1": np.ascontiguousarray(inp["L0_ffn_e0_w1"]),
        "e0_w2": np.ascontiguousarray(inp["L0_ffn_e0_w2"]),
        "e1_w1": np.ascontiguousarray(inp["L0_ffn_e1_w1"]),
        "e1_w2": np.ascontiguousarray(inp["L0_ffn_e1_w2"]),
        "e1_w3": np.ascontiguousarray(inp["L0_ffn_e1_w3"]),
        "e2_w1": np.ascontiguousarray(inp["L0_ffn_e2_w1"]),
        "e2_w2": np.ascontiguousarray(inp["L0_ffn_e2_w2"]),
        "ident128": ident,
        "trimask": tri,
        "sel_heads": sel,
        "ktarg": ktarg,
        "sel8": sel8,
        "e0_b1_rep": np.ascontiguousarray(np.asarray(inp["L0_ffn_e0_b1"]).reshape(3, 32, 128).transpose(2, 0, 1)),
        "e0_b2_rep": np.ascontiguousarray(np.asarray(inp["L0_ffn_e0_b2"]).reshape(3, 8, 128).transpose(2, 0, 1)),
        "e1_b1_rep": np.ascontiguousarray(np.asarray(inp["L0_ffn_e1_b1"]).reshape(3, 16, 128).transpose(2, 0, 1)),
        "e1_b2_rep": np.ascontiguousarray(np.asarray(inp["L0_ffn_e1_b2"]).reshape(3, 16, 128).transpose(2, 0, 1)),
        "e1_b3_rep": np.ascontiguousarray(np.asarray(inp["L0_ffn_e1_b3"]).reshape(3, 8, 128).transpose(2, 0, 1)),
        "e2_b1_rep": np.ascontiguousarray(np.asarray(inp["L0_ffn_e2_b1"]).reshape(2, 24, 128).transpose(2, 0, 1)),
        "e2_b2_rep": np.ascontiguousarray(np.asarray(inp["L0_ffn_e2_b2"]).reshape(2, 8, 128).transpose(2, 0, 1)),
    }
    in_maps = []
    for c in range(NCORES):
        m = dict(shared)
        m["x_t"] = np.ascontiguousarray(x[c].T)
        m["noise"] = np.ascontiguousarray(noise[c])
        in_maps.append(m)

    nc = build_program()
    res = run_bass_kernel_spmd(nc, in_maps, core_ids=list(range(NCORES)))
    outs = []
    for c in range(NCORES):
        o = res.results[c]["out_t"].reshape(DM, T)
        outs.append(np.ascontiguousarray(o.T))
    return np.stack(outs).astype(np.float32)
